# version 05
# baseline (speedup 1.0000x reference)
import numpy as np
import ml_dtypes
from contextlib import ExitStack

import concourse.bass as bass
import concourse.mybir as mybir
from concourse.bass_utils import run_bass_kernel_spmd

F32 = mybir.dt.float32
BF16 = mybir.dt.bfloat16
I32 = mybir.dt.int32
AF = mybir.ActivationFunctionType
ALU = mybir.AluOpType
AX = mybir.AxisListType
BF = ml_dtypes.bfloat16

NCORES = 8
B, S, D = 2, 16384, 2048
T = 4096
NCH = 32
ND = 16
H, HD = 16, 128
M = 4
E = 8
F, NF = 1408, 11
SCALE = HD ** -0.5
LN_EPS = 1e-5
EPS_A = 1e-6
NEG_BIG = -1e30

DEBUG = ()
P3_PASSES = 9


class Prog:
    ENGINES = ("sync", "scalar", "vector", "tensor", "gpsimd")

    CHAIN = ("vector", "scalar", "gpsimd")

    def __init__(self, nc, chain_sems=None):
        self.nc = nc
        self.ops = {k: [] for k in self.ENGINES}
        self.cnt = {}
        self.chain_sem = chain_sems or {}

    def count(self, sem):
        return self.cnt.get(sem.name, 0)

    def op(self, eng, fn, waits=None, inc=None, inc16=None, nochain=False):
        waits = list(waits or [])
        chained = eng in self.CHAIN and self.chain_sem[eng] is not None
        if chained:
            cs = self.chain_sem[eng]
            waits.append((cs, self.count(cs)))
        w = []
        for w_ in waits:
            if w_ is None:
                continue
            (sem, val) = w_
            assert val <= self.count(sem), \
                f"wait on future count {sem.name} {val} > {self.count(sem)}"
            if val > 0:
                w.append((sem, val))
        incs = []
        assert inc is None or inc16 is None
        assert not (chained and not nochain and (inc is not None or inc16 is not None))
        mile = None
        if inc is not None:
            self.cnt[inc.name] = self.count(inc) + 1
            incs.append((inc, 1))
            mile = (inc, self.count(inc))
        elif inc16 is not None:
            self.cnt[inc16.name] = self.count(inc16) + 16
            incs.append((inc16, 16))
            mile = (inc16, self.count(inc16))
        elif chained and not nochain:
            cs = self.chain_sem[eng]
            self.cnt[cs.name] = self.count(cs) + 1
            incs.append((cs, 1))
            mile = (cs, self.count(cs))
        self.ops[eng].append((fn, w, incs))
        return mile

    def dma(self, q, out, in_, sem, waits=None, **kw):
        return self.op(q, (lambda e, o=out, i=in_, kw=kw:
                           e.dma_start(out=o, in_=i, **kw)),
                       waits=waits, inc16=sem, nochain=True)

    def mm(self, out, lhsT, rhs, start, stop, waits=None, inc=None):
        return self.op("tensor",
                       (lambda e, o=out, l=lhsT, r=rhs, st=start, sp=stop:
                        e.matmul(o, l, r, start=st, stop=sp)),
                       waits=waits, inc=inc)

    def tp(self, out, in_, ident, waits=None, inc=None):
        return self.op("tensor",
                       (lambda e, o=out, i=in_, d=ident: e.transpose(o, i, d)),
                       waits=waits, inc=inc)

    def v(self, fn, waits=None, inc=None):
        return self.op("vector", fn, waits=waits, inc=inc)

    def a(self, fn, waits=None, inc=None):
        return self.op("scalar", fn, waits=waits, inc=inc)

    def g(self, fn, waits=None, inc=None, nochain=False):
        return self.op("gpsimd", fn, waits=waits, inc=inc, nochain=nochain)

    def emit(self, block):
        for eng_name in self.ENGINES:
            ops = self.ops[eng_name]

            def body(eng, ops=ops):
                hwm = {}
                for (fn, waits, incs) in ops:
                    for (sem, val) in waits:
                        if val > hwm.get(sem.name, 0):
                            eng.wait_ge(sem, val)
                            hwm[sem.name] = val
                    inst = fn(eng)
                    for (sem, val) in incs:
                        inst = inst.then_inc(sem, val)

            getattr(block, eng_name)(body)


def build():
    nc = bass.Bass("TRN2", target_bir_lowering=False)

    d_x = nc.dram_tensor("x_tm", [T, D], F32, kind="ExternalInput")
    d_xfm = nc.dram_tensor("x_fm", [D, T], F32, kind="ExternalInput")
    d_wq = nc.dram_tensor("wq_t", [D, D], F32, kind="ExternalInput")
    d_wk = nc.dram_tensor("wk_t", [D, D], F32, kind="ExternalInput")
    d_wv = nc.dram_tensor("wv_t", [D, D], F32, kind="ExternalInput")
    d_wo = nc.dram_tensor("wo_t", [D, D], F32, kind="ExternalInput")
    d_mem = nc.dram_tensor("memories", [M, H, HD, HD], F32, kind="ExternalInput")
    d_mnorm = nc.dram_tensor("memnorms_t", [HD, H * M], F32, kind="ExternalInput")
    d_gate = nc.dram_tensor("gate_row", [1, H], F32, kind="ExternalInput")
    d_gw = nc.dram_tensor("gate_wT", [D, E], F32, kind="ExternalInput")
    d_ebias = nc.dram_tensor("ebias_row", [1, E], F32, kind="ExternalInput")
    d_w1 = nc.dram_tensor("w1t", [E, D, F], F32, kind="ExternalInput")
    d_w3 = nc.dram_tensor("w3t", [E, D, F], F32, kind="ExternalInput")
    d_w2 = nc.dram_tensor("w2t", [E, F, D], F32, kind="ExternalInput")
    d_sw1 = nc.dram_tensor("sw1t", [D, F], F32, kind="ExternalInput")
    d_sw3 = nc.dram_tensor("sw3t", [D, F], F32, kind="ExternalInput")
    d_sw2 = nc.dram_tensor("sw2t", [F, D], F32, kind="ExternalInput")
    d_identb = nc.dram_tensor("identb", [128, 128], BF16, kind="ExternalInput")
    d_identf = nc.dram_tensor("identf", [128, 128], F32, kind="ExternalInput")
    d_trimask = nc.dram_tensor("trimask", [128, 128], BF16, kind="ExternalInput")
    d_E16 = nc.dram_tensor("E16", [16, 16 * 128], F32, kind="ExternalInput")
    d_E8 = nc.dram_tensor("E8", [8, 8 * 128], BF16, kind="ExternalInput")
    d_prefmask = nc.dram_tensor("prefmask", [1, 8], F32, kind="ExternalInput")
    d_relmask = nc.dram_tensor("relmask", [1, 8], F32, kind="ExternalInput")

    d_xnfm = nc.dram_tensor("xnfm", [ND, 128, T], BF16)
    d_qkv = nc.dram_tensor("qkv", [H, 8, 128, 2052], BF16)
    d_attn = nc.dram_tensor("attnfm", [H, 128, T], BF16)
    d_y = nc.dram_tensor("ydram", [T, D], BF16)
    d_ccin = nc.dram_tensor("ccin", [128, 2080], F32)
    d_ccout = nc.dram_tensor("ccout", [8 * 128, 2080], F32)

    d_out = nc.dram_tensor("out", [T, D], F32, kind="ExternalOutput")
    dbg = {}
    for name, shape in (("xn", [T, D]), ("scores", [T, E]), ("alphas", [T, E]),
                        ("yffn", [T, D]), ("hsb", [128, 11 * 2048]), ("xin0", [128, 16 * 2048]),
                        ("attn", [H, 128, T]), ("den0", [128, 64]),
                        ("acc0", [128, 512]), ("sbf0", [128, 129]), ("spsum", [128, 129]),
                        ("vals8", [T, E]), ("ind", [T, E]),
                        ("xnsum", [128, ND]), ("alphasT", [8, T]),
                        ("qkv0", [8, 128, 2052]), ("kvz", [128, 2080]),
                        ("sbase", [128, 2064]), ("wts", [16, 4]),
                        ("arep", [128, 256]), ("bg", [128, 16])):
        if name in DEBUG:
            dbg[name] = nc.dram_tensor("dbg_" + name, shape, F32, kind="ExternalOutput")

    with ExitStack() as ctx:
        def sbuf(name, shape, dt=F32):
            return ctx.enter_context(nc.sbuf_tensor("sb_" + name, shape, dt))

        def psum(name, shape, dt=F32):
            return ctx.enter_context(nc.psum_tensor(name, shape, dt))

        def sem(name):
            return ctx.enter_context(nc.semaphore(name))

        identb = sbuf("identb", [128, 128], BF16)
        identf = sbuf("identf", [128, 128], F32)
        trimask = sbuf("trimask", [128, 128], BF16)
        ones_f = sbuf("ones_f", [1, 128], F32)
        ones_c = sbuf("ones_c", [128, 1], F32)
        ones_bf = sbuf("ones_bf", [128, 1], BF16)
        one_f = sbuf("one_f", [1, 1], F32)
        gate_row = sbuf("gate_row_sb", [1, H], F32)
        ebias_row = sbuf("ebias_sb", [1, E], F32)
        gw_sb = sbuf("gw_sb", [128, ND * E], F32)
        bias_bc = sbuf("bias_bc", [128, E], F32)
        cs_bc = sbuf("cs_bc", [128, E], F32)
        alphas = sbuf("alphas", [128, NCH * E], F32)
        alphasT = sbuf("alphasT", [8, T], BF16)
        xnsum = sbuf("xnsum", [128, ND], F32)
        xnsum_bf = sbuf("xnsum_bf", [128, ND], BF16)
        qsum_bf = sbuf("qsum_bf", [128, H], BF16)
        norms_bf = sbuf("norms_bf", [128, H * M], BF16)
        A_rep = sbuf("A_rep", [128, H * 16], F32)
        BG = sbuf("BG", [128, H], F32)
        S_bf = [sbuf("S_bf0", [128, 129], BF16), sbuf("S_bf1", [128, 129], BF16)]
        E8_sb = sbuf("E8_sb", [8, 8 * 128], BF16)
        prefm = sbuf("prefm", [1, 8], F32)
        relm = sbuf("relm", [1, 8], F32)
        small = sbuf("small", [16, 128], F32)
        relw = sbuf("relw", [16, 16], F32)

        AR = sbuf("AR", [128, 181 * 1024], mybir.dt.uint8)
        AR_M = ("seg", 0)
        AR_A = ("seg", 34816)
        AR_W = ("seg", 108544)

        def carve(arena, off_bytes, shape, dt):
            if isinstance(arena, tuple):
                off_bytes += arena[1]
            esz = {F32: 4, BF16: 2, I32: 4}[dt]
            assert off_bytes % esz == 0
            ap = AR.ap().bitcast(dt)
            n = int(np.prod(shape))
            v = ap[:, off_bytes // esz: off_bytes // esz + n]
            if len(shape) > 1:
                pat = " ".join(f"s{i}" for i in range(len(shape)))
                v = v.rearrange(f"p ({pat}) -> p {pat}",
                                **{f"s{i}": int(shape[i]) for i in range(len(shape))})
            return v

        pb = [psum(f"pb{i}", [128, 512], F32) for i in range(7)]
        pbf = psum("pbbf", [128, 1024], BF16)

        s_ld = sem("s_ld")
        s_ldg = sem("s_ldg")
        s_st = sem("s_st")
        s_stg = sem("s_stg")
        s_pe = sem("s_pe")
        s_ev = sem("s_ev")
        s_ac = sem("s_ac")
        s_po = sem("s_po")
        s_cc = sem("s_cc")
        s_out = sem("s_out")
        s_vf = sem("s_vf")
        s_pf = sem("s_pf")

        p = Prog(nc, chain_sems={"vector": s_vf, "scalar": s_ac,
                                 "gpsimd": s_pf})

        lds = []
        for dst, src in ((identb, d_identb), (identf, d_identf), (trimask, d_trimask),
                         (gate_row, d_gate), (ebias_row, d_ebias),
                         (prefm, d_prefmask), (relm, d_relmask)):
            lds.append(p.dma("sync", dst[:], src.ap(), s_ld))
        p.dma("sync", E8_sb[:], d_E8.ap(), s_ld)
        ld0 = p.dma("sync", gw_sb[:].rearrange("p (k e) -> p k e", e=E),
                    d_gw.ap().rearrange("(k p) e -> p k e", p=128), s_ld)
        ldg_norm = p.dma("gpsimd", norms_bf[:], d_mnorm.ap(), s_ldg)

        p.v(lambda e: e.memset(ones_f[:], 1.0))
        p.v(lambda e: e.memset(ones_c[:], 1.0))
        p.v(lambda e: e.memset(ones_bf[:], 1.0))
        p.v(lambda e: e.memset(one_f[:], 1.0))
        ev0 = p.v(lambda e: e.memset(xnsum[:], 0.0))

        p.mm(pb[0][:, 0:E], ones_f[:], ebias_row[:], True, True,
             waits=[ld0, ev0])
        for k in range(ND):
            pe_cs = p.mm(pb[1][0:1, 0:E], ones_c[:, 0:1], gw_sb[:, k * E:(k + 1) * E],
                         k == 0, k == ND - 1, inc=(s_pe if k == ND - 1 else None))
        p.v(lambda e: e.tensor_copy(out=bias_bc[:], in_=pb[0][:, 0:E]),
            waits=[pe_cs])
        ev_b = p.v(lambda e: e.tensor_copy(out=small[0:1, 0:E], in_=pb[1][0:1, 0:E]))
        pe_cb = p.mm(pb[0][:, 8:16], ones_f[:], small[0:1, 0:E], True, True,
                     waits=[ev_b], inc=s_pe)
        ev_cs = p.v(lambda e: e.tensor_copy(out=cs_bc[:], in_=pb[0][:, 8:16]),
                    waits=[pe_cb])

        xch = [carve(AR_M, 0, (2048,), F32), carve(AR_M, 8192, (2048,), F32)]
        xfmch = [carve(AR_M, 16384, (16, 128), F32), carve(AR_M, 24576, (16, 128), F32)]
        sqdump = carve(AR_A, 0, (2048,), F32)
        xnbf = carve(AR_A, 8192, (2048,), BF16)
        xnfmc = [carve(AR_A, 12288, (16, 128), BF16), carve(AR_A, 16384, (16, 128), BF16)]
        stat = carve(AR_A, 20480, (16,), F32)
        sc8 = carve(AR_A, 20544, (48,), F32)
        albf = carve(AR_A, 20736, (8,), BF16)
        xns_t = carve(AR_A, 20752, (16,), F32)
        dbgbuf = carve(AR_A, 24576, (2048,), F32)

        done_xc = [None, None]
        done_xf = [None, None]
        done_xnfm = [None, None]
        done_xnbf = None
        done_albf = None
        for c in range(NCH):
            bp = c % 2
            p.dma("sync", xch[bp], d_x.ap()[c * 128:(c + 1) * 128, :], s_ld,
                  waits=([done_xc[bp]] if done_xc[bp] else []))
            ldc = p.dma("sync", xfmch[bp],
                        d_xfm.ap()[:, c * 128:(c + 1) * 128]
                        .rearrange("(dt p) c -> p dt c", p=128), s_ld,
                        waits=([done_xf[bp]] if done_xf[bp] else []))

            xc = xch[bp]
            xf = xfmch[bp]
            mu = stat[:, 0:1]; rstd = stat[:, 1:2]; nrstd = stat[:, 2:3]
            ssq = stat[:, 3:4]; t0 = stat[:, 4:5]
            p.v(lambda e, xc=xc, t0=t0: e.reduce_sum(out=t0, in_=xc, axis=AX.X),
                waits=[ldc, ev_cs])
            p.v(lambda e, t0=t0, mu=mu: e.tensor_scalar(
                out=mu, in0=t0, scalar1=1.0 / D, scalar2=None, op0=ALU.mult))
            ev_xc = p.v(lambda e, xc=xc, mu=mu: e.tensor_scalar(
                out=xc, in0=xc, scalar1=mu, scalar2=None, op0=ALU.subtract))
            ac_sq = p.a(lambda e, xc=xc, ssq=ssq: e.activation(
                out=sqdump, in_=xc, func=AF.Square, accum_out=ssq),
                waits=[ev_xc])
            p.v(lambda e, ssq=ssq, t0=t0: e.tensor_scalar(
                out=t0, in0=ssq, scalar1=1.0 / D, scalar2=LN_EPS,
                op0=ALU.mult, op1=ALU.add), waits=[ac_sq])
            ev_t0 = p.v(lambda e: e.memset(small[0:1, 32:33], 0.0))
            ac_rt = p.a(lambda e, t0=t0: e.sqrt(out=t0, in_=t0), waits=[ev_t0])
            p.v(lambda e, t0=t0, rstd=rstd: e.reciprocal(out=rstd, in_=t0),
                waits=[ac_rt])
            p.v(lambda e, rstd=rstd, nrstd=nrstd: e.tensor_scalar(
                out=nrstd, in0=rstd, scalar1=-1.0, scalar2=None, op0=ALU.mult))
            ev_xn = p.v(lambda e, xc=xc, rstd=rstd: e.tensor_scalar(
                out=xnbf, in0=xc, scalar1=rstd, scalar2=None, op0=ALU.mult),
                waits=([done_xnbf] if done_xnbf else []))
            done_xc[bp] = ev_xn

            for k in range(ND):
                pe_u = p.mm(pb[2][:, 0:E], xf[:, k, :], gw_sb[:, k * E:(k + 1) * E],
                            k == 0, k == ND - 1,
                            waits=([ldc, ev_cs] if k == 0 else None),
                            inc=(s_pe if k == ND - 1 else None))
            done_xf[bp] = pe_u
            vsel = sc8[:, 0:8]; sco = sc8[:, 8:16]; vals8 = sc8[:, 16:24]
            ind = sc8[:, 24:32]
            ev_t = p.v(lambda e, vsel=vsel, mu=mu: e.scalar_tensor_tensor(
                out=vsel, in0=cs_bc[:], scalar=mu, in1=pb[2][:, 0:E],
                op0=ALU.mult, op1=ALU.subtract), waits=[pe_u])
            ac_sg = p.a(lambda e, sco=sco, vsel=vsel, nrstd=nrstd: e.activation(
                out=sco, in_=vsel, func=AF.Sigmoid, scale=nrstd),
                waits=[ev_t])
            p.v(lambda e, vsel=vsel, sco=sco: e.tensor_add(
                out=vsel, in0=sco, in1=bias_bc[:]), waits=[ac_sg])
            p.v(lambda e, vals8=vals8, vsel=vsel: e.max(out=vals8, in_=vsel))
            p.v(lambda e, ind=ind, vsel=vsel, vals8=vals8: e.tensor_scalar(
                out=ind, in0=vsel, scalar1=vals8[:, 1:2], scalar2=None, op0=ALU.is_ge))
            ev_al = p.v(lambda e, ind=ind, sco=sco, c=c: e.tensor_mul(
                out=alphas[:, c * E:(c + 1) * E], in0=ind, in1=sco))

            ac_ab = p.a(lambda e, c=c: e.copy(
                out=albf, in_=alphas[:, c * E:(c + 1) * E]),
                waits=[ev_al] + ([done_albf] if done_albf else []))
            pe_at = p.tp(pbf[0:8, 0:128], albf, identb[:],
                         waits=[(s_ac, p.count(s_ac))], inc=s_pe)
            done_albf = pe_at
            ac_at = p.a(lambda e, c=c: e.copy(
                out=alphasT[:, c * 128:(c + 1) * 128], in_=pbf[0:8, 0:128]),
                waits=[pe_at])

            wspill = [done_xnfm[bp]] if done_xnfm[bp] else []
            for half in range(2):
                for t8 in range(8):
                    dt_i = half * 8 + t8
                    wts_ = None
                    if t8 == 0:
                        wts_ = ([ev_xn, (s_ac, p.count(s_ac))]
                                + (wspill if half == 0 else []))
                    pe_t = p.tp(pbf[:, t8 * 128:(t8 + 1) * 128],
                                xnbf[:, dt_i * 128:(dt_i + 1) * 128],
                                identb[:], waits=wts_,
                                inc=(s_pe if t8 == 7 else None))
                ac_e = p.a(lambda e, half=half, bp=bp: e.copy(
                    out=xnfmc[bp][:, half * 8:(half + 1) * 8, :]
                    .rearrange("p a b -> p (a b)"),
                    in_=pbf[:, 0:1024]), waits=[pe_t])
            done_xnbf = pe_t
            ac_t = (s_ac, p.count(s_ac))
            p.v(lambda e, bp=bp: e.tensor_reduce(
                out=xns_t, in_=xnfmc[bp], axis=AX.X, op=ALU.add),
                waits=[ac_t])
            p.v(lambda e: e.tensor_add(out=xnsum[:], in0=xnsum[:], in1=xns_t))
            st_x = p.dma("sync", d_xnfm.ap()[:, :, c * 128:(c + 1) * 128]
                         .rearrange("dt p c -> p dt c"),
                         xnfmc[bp], s_st, waits=[ac_t])
            done_xnfm[bp] = st_x

            if "xn" in dbg:
                ev_d = p.v(lambda e: e.tensor_copy(out=dbgbuf, in_=xnbf), waits=[ev_xn])
                p.dma("sync", dbg["xn"].ap()[c * 128:(c + 1) * 128, :], dbgbuf, s_st,
                      waits=[ev_d])
            if "scores" in dbg:
                p.dma("sync", dbg["scores"].ap()[c * 128:(c + 1) * 128, :], sco, s_st,
                      waits=[ev_al])
            if "vals8" in dbg:
                p.dma("sync", dbg["vals8"].ap()[c * 128:(c + 1) * 128, :], vals8, s_st,
                      waits=[ev_al])
                p.dma("sync", dbg["ind"].ap()[c * 128:(c + 1) * 128, :], ind, s_st,
                      waits=[ev_al])
            if "alphas" in dbg:
                p.dma("sync", dbg["alphas"].ap()[c * 128:(c + 1) * 128, :],
                      alphas[:, c * E:(c + 1) * E], s_st, waits=[ev_al])

        ev_last = p.v(lambda e: e.memset(small[0:1, 33:34], 0.0))
        ac_xs = p.a(lambda e: e.copy(out=xnsum_bf[:], in_=xnsum[:]),
                    waits=[ev_last])
        if "xnsum" in dbg:
            p.dma("sync", dbg["xnsum"].ap(), xnsum[:], s_st, waits=[ev_last])
        if "alphasT" in dbg:
            ev_d = p.v(lambda e: e.tensor_copy(
                out=carve(AR_M, 0, (4096,), F32)[0:8, :], in_=alphasT[:]),
                waits=[(s_ac, p.count(s_ac))])
            p.dma("sync", dbg["alphasT"].ap(), carve(AR_M, 0, (4096,), F32)[0:8, :],
                  s_st, waits=[ev_d])

        p1_bar = [done_xnfm[0], done_xnfm[1], done_xnbf, ac_xs]

        wqkv = carve(AR_W, 0, (3, 4, 16, 128), BF16)
        xng = [carve(AR_A, 0, (16, 512), BF16),
               carve(AR_A, 16384, (16, 512), BF16)]
        stage = [carve(AR_A, 32768, (2052,), BF16),
                 carve(AR_A, 36872, (2052,), BF16)]
        tmpA = carve(AR_A, 40976, (512,), F32)
        tmpB = carve(AR_A, 43024, (512,), F32)
        cc_sb = carve(AR_A, 45072, (2080,), F32)

        for i in range(2):
            for ci in range(4):
                p.v(lambda e, i=i, ci=ci: e.memset(
                    stage[i][:, 1536 + ci * 129 + 128: 1536 + ci * 129 + 129], 1.0),
                    waits=(p1_bar if (i == 0 and ci == 0) else None))

        done_xng = [None, None]
        done_stage = [None, None]
        done_w = [None, None, None]
        done_pb = {0: None, 1: None, 2: None}
        done_kvz = None
        ac_qs = None
        for hs in range(4):
            heads = [hs * 4 + i for i in range(4)]
            wld = None
            for mi, dw in enumerate((d_wq, d_wk, d_wv)):
                for hh in range(4):
                    h = heads[hh]
                    wld = p.dma("gpsimd", wqkv[:, mi, hh],
                                dw.ap()[:, h * 128:(h + 1) * 128]
                                .rearrange("(k p) c -> p k c", p=128),
                                s_ldg,
                                waits=((p1_bar if hs == 0 else []) +
                                       ([done_w[mi]] if done_w[mi] else [])
                                       if hh == 0 else None))
            for hh in range(4):
                for k in range(ND):
                    pe_qs = p.mm(pb[2][:, hh:hh + 1], wqkv[:, 0, hh, k],
                                 xnsum_bf[:, k:k + 1],
                                 hh == 0 and k == 0, hh == 3 and k == ND - 1,
                                 waits=([wld, ac_xs] +
                                        ([done_pb[2]] if done_pb[2] else [])
                                        if (hh == 0 and k == 0) else None),
                                 inc=(s_pe if (hh == 3 and k == ND - 1) else None))
            ac_qs = p.a(lambda e, hs=hs: e.copy(
                out=qsum_bf[:, hs * 4:hs * 4 + 4], in_=pb[2][:, 0:4]),
                waits=[pe_qs])
            done_pb[2] = ac_qs

            pe_kvg = None
            for g in range(8):
                bp = g % 2
                ldx = p.dma("sync", xng[bp],
                            d_xnfm.ap()[:, :, g * 512:(g + 1) * 512]
                            .rearrange("dt p c -> p dt c"), s_ld,
                            waits=([done_xng[bp]] if done_xng[bp]
                                   else (p1_bar if hs == 0 else [])))
                pe_last_v = None
                for hh in range(4):
                    h = heads[hh]
                    sp = (g * 4 + hh) % 2
                    wst = ([x for x in (done_stage[sp],) if x] or [])
                    for k in range(ND):
                        pe_q = p.mm(pb[0][:, 0:512], wqkv[:, 0, hh, k],
                                    xng[bp][:, k, :], k == 0, k == ND - 1,
                                    waits=(([ldx, wld] +
                                            ([done_pb[0]] if done_pb[0] else []))
                                           if k == 0 else None),
                                    inc=(s_pe if k == ND - 1 else None))
                    ev_m = p.v(lambda e: e.tensor_scalar(
                        out=tmpA, in0=pb[0][:, 0:512], scalar1=0.0, scalar2=None,
                        op0=ALU.min), waits=[pe_q] + wst)
                    ac_e = p.a(lambda e: e.activation(out=tmpB, in_=tmpA, func=AF.Exp),
                               waits=[ev_m])
                    ev_qp = p.v(lambda e, sp=sp: e.scalar_tensor_tensor(
                        out=stage[sp][:, 0:512], in0=pb[0][:, 0:512], scalar=0.0,
                        in1=tmpB, op0=ALU.max, op1=ALU.add), waits=[ac_e])
                    done_pb[0] = ev_qp
                    for k in range(ND):
                        pe_k = p.mm(pb[1][:, 0:512], wqkv[:, 1, hh, k],
                                    xng[bp][:, k, :], k == 0, k == ND - 1,
                                    waits=(([done_pb[1]] if done_pb[1] else [])
                                           if k == 0 else None),
                                    inc=(s_pe if k == ND - 1 else None))
                    ev_m = p.v(lambda e: e.tensor_scalar(
                        out=tmpA, in0=pb[1][:, 0:512], scalar1=0.0, scalar2=None,
                        op0=ALU.min), waits=[pe_k])
                    ac_e = p.a(lambda e: e.activation(out=tmpB, in_=tmpA, func=AF.Exp),
                               waits=[ev_m])
                    ev_kp = p.v(lambda e, sp=sp: e.scalar_tensor_tensor(
                        out=stage[sp][:, 512:1024], in0=pb[1][:, 0:512], scalar=0.0,
                        in1=tmpB, op0=ALU.max, op1=ALU.add), waits=[ac_e])
                    done_pb[1] = ev_kp
                    for ci in range(4):
                        pe_tp = p.tp(pbf[:, ci * 128:(ci + 1) * 128],
                                     stage[sp][:, 512 + ci * 128:512 + (ci + 1) * 128],
                                     identb[:],
                                     waits=([ev_kp, (s_ac, p.count(s_ac))]
                                            if ci == 0 else None),
                                     inc=(s_pe if ci == 3 else None))
                    ac_tp = p.a(lambda e, sp=sp: e.copy(
                        out=stage[sp][:, 1024:1536], in_=pbf[:, 0:512]),
                        waits=[pe_tp])
                    ac_v = None
                    for cpair in range(2):
                        for ci2 in range(2):
                            ci = cpair * 2 + ci2
                            for k in range(ND):
                                pe_v = p.mm(pb[2][:, ci2 * 128:(ci2 + 1) * 128],
                                            xng[bp][:, k, ci * 128:(ci + 1) * 128],
                                            wqkv[:, 2, hh, k],
                                            k == 0 and ci2 == 0, k == ND - 1,
                                            waits=(([done_pb[2]]
                                                    if (done_pb[2] and ci2 == 0)
                                                    else None) if k == 0 else None),
                                            inc=(s_pe if (k == ND - 1 and ci2 == 1)
                                                 else None))
                        ac_v = p.a(lambda e, sp=sp, cpair=cpair: e.copy(
                            out=stage[sp][:, 1536 + cpair * 258:
                                          1536 + (cpair + 1) * 258]
                            .rearrange("p (a b) -> p a b", b=129)[:, :, 0:128],
                            in_=pb[2][:, 0:256]
                            .rearrange("p (a b) -> p a b", b=128)), waits=[pe_v])
                        done_pb[2] = ac_v
                    last_q, last_k, last_v = pe_q, pe_k, pe_v
                    for ci in range(4):
                        pe_kvg = p.mm(pb[3 + hh][:, 0:129],
                                      stage[sp][:, 1024 + ci * 128:1024 + (ci + 1) * 128],
                                      stage[sp][:, 1536 + ci * 129:1536 + (ci + 1) * 129],
                                      g == 0 and ci == 0, g == 7 and ci == 3,
                                      waits=(([ac_tp, ac_v] +
                                              ([done_kvz] if (g == 0 and ci == 0)
                                               else []))
                                             if ci == 0 else None),
                                      inc=(s_pe if ci == 3 else None))
                    st_sp = p.dma("sync", d_qkv.ap()[h, g], stage[sp], s_st,
                                  waits=[ev_qp, ac_tp, ac_v, pe_kvg])
                    done_stage[sp] = st_sp
                    pe_last_v = pe_kvg
                done_xng[bp] = pe_last_v
            done_w = [last_q, last_k, last_v]
            ev_kvz = None
            for hh in range(4):
                ev_kvz = p.v(lambda e, hs=hs, hh=hh: e.tensor_copy(
                    out=cc_sb[:, (hs * 4 + hh) * 129:(hs * 4 + hh + 1) * 129],
                    in_=pb[3 + hh][:, 0:129]), waits=[pe_last_v])
            done_kvz = ev_kvz

        for h in range(H):
            pe_rel = p.mm(pb[0][0:4, h:h + 1], norms_bf[:, h * 4:(h + 1) * 4],
                          qsum_bf[:, h:h + 1], h == 0, h == H - 1,
                          waits=([ldg_norm, ac_qs, done_pb[0]] if h == 0 else None),
                          inc=(s_pe if h == H - 1 else None))
        ev_rel = p.v(lambda e: e.tensor_copy(
            out=cc_sb[0:4, 2064:2080], in_=pb[0][0:4, 0:16]), waits=[pe_rel])
        st_cc = p.dma("sync", d_ccin.ap(), cc_sb, s_st, waits=[ev_rel])
        cc_done = p.g(lambda e: e.collective_compute(
            "AllGather", ALU.bypass, replica_groups=[list(range(NCORES))],
            ins=[d_ccin.ap().opt()], outs=[d_ccout.ap().opt()]),
            waits=[st_cc], inc=s_cc, nochain=True)

        if "qkv0" in dbg:
            p.dma("gpsimd", dbg["qkv0"].ap(), d_qkv.ap()[0], s_st,
                  waits=[(s_st, p.count(s_st))])
        if "kvz" in dbg:
            p.dma("sync", dbg["kvz"].ap(), cc_sb, s_st,
                  waits=[ev_rel])

        p3_bar = [(s_st, p.count(s_st)), (s_ldg, p.count(s_ldg)),
                  (s_pe, p.count(s_pe)), (s_ac, p.count(s_ac)),
                  (s_vf, p.count(s_vf))]
        w2_bf = carve(AR, 0, (11, 2048), BF16)
        w13 = [carve(AR, 46080, (2, 16, 128), BF16),
               carve(AR, 54272, (2, 16, 128), BF16)]
        B_sb = carve(AR, 62464, (4, 512), BF16)
        tmpS = carve(AR, 66560, (512,), F32)
        yev = carve(AR, 68608, (2048,), BF16)
        xnh = carve(AR, 73728, (16, 2048), BF16)
        h_sb = carve(AR, 139264, (11, 2048), BF16)

        y_done = [None] * NCH
        done_pbfB = None
        done_Bsb = []
        done_hsb = None
        done_yev = None
        done_w2 = None
        done_w13 = [None, None]
        done_xnh = None
        done_p3pb = [None] * 8
        for ei in range(P3_PASSES):
            routed = ei > 0
            e_idx = ei - 1
            src_w1 = d_w1.ap()[e_idx] if routed else d_sw1.ap()
            src_w3 = d_w3.ap()[e_idx] if routed else d_sw3.ap()
            src_w2 = d_w2.ap()[e_idx] if routed else d_sw2.ap()
            ld_w2 = p.dma("gpsimd", w2_bf,
                          src_w2.rearrange("(k p) c -> p k c", p=128), s_ldg,
                          waits=(p3_bar if ei == 0 else [done_w2]))
            for half in range(2):
                ldxn = p.dma("sync", xnh,
                             d_xnfm.ap()[:, :, half * 2048:(half + 1) * 2048]
                             .rearrange("dt p c -> p dt c"), s_ld,
                             waits=([done_xnh] if done_xnh else p3_bar))
                ev_scale = None
                po_scale = None
                if routed:
                    ac_B = None
                    pbfF = pbf.ap().bitcast(F32)
                    for g4 in range(4):
                        g = half * 4 + g4
                        pe_B = p.mm(pbfF[:, 0:512],
                                    E8_sb[:, e_idx * 128:(e_idx + 1) * 128],
                                    alphasT[:, g * 512:(g + 1) * 512],
                                    True, True,
                                    waits=([done_pbfB] if (done_pbfB and g4 == 0)
                                           else ([ac_B] if ac_B else None)),
                                    inc=s_pe)
                        ac_B = p.a(lambda e, g4=g4, pbfF=pbfF: e.copy(
                            out=B_sb[:, g4, :], in_=pbfF[:, 0:512]),
                            waits=[pe_B] + (done_Bsb if g4 == 0 else []))
                    done_pbfB = ac_B
                    for k in range(ND):
                        for g4 in range(4):
                            fn = (lambda e, k=k, g4=g4: e.tensor_mul(
                                out=xnh[:, k, g4 * 512:(g4 + 1) * 512],
                                in0=xnh[:, k, g4 * 512:(g4 + 1) * 512],
                                in1=B_sb[:, g4, :]))
                            wt = [ldxn, ac_B] if g4 == 0 and k <= 1 else None
                            if k % 2 == 0:
                                ev_scale = p.v(fn, waits=wt)
                            else:
                                po_scale = p.g(fn, waits=wt)
                    done_Bsb = [ev_scale, po_scale]
                scale_bar = [m for m in (ldxn, ev_scale, po_scale) if m]
                if "xin0" in dbg and ei == 1 and half == 0:
                    p.dma("gpsimd", dbg["xin0"].ap(),
                          xnh.rearrange("p a b -> p (a b)"), s_st, waits=scale_bar)
                pe_h_last = None
                ev_h = None
                for f in range(NF):
                    wp = f % 2
                    p.dma("gpsimd", w13[wp][:, 0],
                          src_w1[:, f * 128:(f + 1) * 128]
                          .rearrange("(k p) c -> p k c", p=128), s_ldg,
                          waits=([done_w13[wp]] if done_w13[wp] else None))
                    ld13 = p.dma("gpsimd", w13[wp][:, 1],
                                 src_w3[:, f * 128:(f + 1) * 128]
                                 .rearrange("(k p) c -> p k c", p=128), s_ldg)
                    pA = pb[0 + 2 * (f % 2)]
                    pB = pb[1 + 2 * (f % 2)]
                    for g4 in range(4):
                        for k in range(ND):
                            w0_ = ([ld13] + scale_bar +
                                   [done_p3pb[0 + 2 * (f % 2)]]) if k == 0 else None
                            p.mm(pA[:, 0:512], w13[wp][:, 0, k],
                                 xnh[:, k, g4 * 512:(g4 + 1) * 512],
                                 k == 0, k == ND - 1, waits=w0_,
                                 inc=(s_pe if k == ND - 1 else None))
                        pe1 = (s_pe, p.count(s_pe))
                        for k in range(ND):
                            w0_ = ([done_p3pb[1 + 2 * (f % 2)]]) if k == 0 else None
                            pe_h_last = p.mm(pB[:, 0:512], w13[wp][:, 1, k],
                                             xnh[:, k, g4 * 512:(g4 + 1) * 512],
                                             k == 0, k == ND - 1, waits=w0_,
                                             inc=(s_pe if k == ND - 1 else None))
                        ac_si = p.a(lambda e, pA=pA: e.activation(
                            out=tmpS, in_=pA[:, 0:512], func=AF.Silu),
                            waits=[pe1] + ([ev_h] if ev_h else []))
                        ev_h = p.v(lambda e, f=f, g4=g4, pB=pB: e.tensor_tensor(
                            out=h_sb[:, f, g4 * 512:(g4 + 1) * 512], in0=tmpS,
                            in1=pB[:, 0:512], op=ALU.mult),
                            waits=[ac_si, pe_h_last] +
                                  ([done_hsb] if (done_hsb and f == 0 and g4 == 0)
                                   else []))
                        done_p3pb[0 + 2 * (f % 2)] = ac_si
                        done_p3pb[1 + 2 * (f % 2)] = ev_h
                    done_w13[wp] = pe_h_last
                done_xnh = pe_h_last
                for chh in range(16):
                    ch = half * 16 + chh
                    bs = 3 * (chh % 2)
                    pey = [None] * 4
                    for o in range(4):
                        for k in range(NF):
                            w0_ = None
                            if k == 0:
                                w0_ = [ev_h, ld_w2] +                                       ([done_p3pb[bs + o]] if done_p3pb[bs + o] else [])
                            pey[o] = p.mm(pb[bs + o][:, 0:512],
                                          h_sb[:, k, chh * 128:(chh + 1) * 128],
                                          w2_bf[:, k, o * 512:(o + 1) * 512],
                                          k == 0, k == NF - 1, waits=w0_,
                                          inc=(s_pe if k == NF - 1 else None))
                    evo = None
                    for o in range(4):
                        wy = [pey[o]] + ([done_yev] if done_yev else [])
                        if o < 2:
                            evo = p.v(lambda e, bs=bs, o=o: e.tensor_copy(
                                out=yev[:, o * 512:(o + 1) * 512],
                                in_=pb[bs + o][:, 0:512]), waits=wy)
                        else:
                            evo = p.a(lambda e, bs=bs, o=o: e.copy(
                                out=yev[:, o * 512:(o + 1) * 512],
                                in_=pb[bs + o][:, 0:512]), waits=wy)
                        done_p3pb[bs + o] = evo
                    yw = [evo, done_p3pb[bs], done_p3pb[bs + 1]]
                    if y_done[ch]:
                        yw.append(y_done[ch])
                    if routed:
                        y_done[ch] = p.dma("gpsimd",
                                           d_y.ap()[ch * 128:(ch + 1) * 128, :], yev,
                                           s_stg, waits=yw, accum_op=ALU.add)
                    else:
                        y_done[ch] = p.dma("sync",
                                           d_y.ap()[ch * 128:(ch + 1) * 128, :], yev,
                                           s_st, waits=yw)
                    done_yev = y_done[ch]
                done_hsb = pey[3]
            done_w2 = pey[3]

        if "hsb" in dbg:
            p.dma("gpsimd", dbg["hsb"].ap(), h_sb.rearrange("p a b -> p (a b)"), s_st,
                  waits=[(s_pe, p.count(s_pe)), (s_vf, p.count(s_vf))])
        if "yffn" in dbg:
            p.dma("gpsimd", dbg["yffn"].ap(), d_y.ap(), s_st,
                  waits=[m for m in y_done if m])

        cp_bar = [cc_done, (s_st, p.count(s_st)), (s_pe, p.count(s_pe)),
                  (s_vf, p.count(s_vf)), (s_ac, p.count(s_ac)),
                  (s_pf, p.count(s_pf)), (s_ldg, p.count(s_ldg))]
        kvz_base = carve(AR, 0, (2064,), F32)
        kvz_tmp = carve(AR, 8320, (2080,), F32)
        E16_sb = carve(AR, 16640, (2048,), F32)

        ld_e16 = p.dma("sync", E16_sb[0:16, :], d_E16.ap(), s_ld, waits=cp_bar)
        ev0m = p.v(lambda e: e.memset(kvz_base, 0.0), waits=cp_bar)
        ev_ra = p.v(lambda e: e.memset(small[0:4, 16:32], 0.0))
        pe_pm = p.mm(pb[0][:, 0:8], ones_f[:], prefm[:], True, False, waits=cp_bar)
        pe_pm = p.mm(pb[0][:, 8:16], ones_f[:], relm[:], False, True, inc=s_pe)
        ev_pm = p.v(lambda e: e.tensor_copy(out=small[0:128, 0:16]
                    if False else cs_bc[:, 0:8], in_=pb[0][:, 0:8]), waits=[pe_pm])
        ev_rm = p.v(lambda e: e.tensor_copy(out=bias_bc[:, 0:8], in_=pb[0][:, 8:16]))
        ld_t = None
        for r in range(NCORES):
            ld_t = p.dma("sync", kvz_tmp,
                         d_ccout.ap()[r * 128:(r + 1) * 128, :], s_ld,
                         waits=[(s_vf, p.count(s_vf))])
            p.v(lambda e, r=r: e.scalar_tensor_tensor(
                out=kvz_base, in0=kvz_tmp[:, 0:2064], scalar=cs_bc[:, r:r + 1],
                in1=kvz_base, op0=ALU.mult, op1=ALU.add), waits=[ld_t, ev_pm])
            ev_ra = p.v(lambda e, r=r: e.scalar_tensor_tensor(
                out=small[0:4, 16:32], in0=kvz_tmp[0:4, 2064:2080],
                scalar=bias_bc[0:4, r:r + 1], in1=small[0:4, 16:32],
                op0=ALU.mult, op1=ALU.add), waits=[ev_rm])
        pe_ns = p.mm(pb[1][0:1, 0:64], ones_bf[:], norms_bf[:], True, True,
                     waits=cp_bar, inc=s_pe)
        ev_ = p.v(lambda e: e.tensor_copy(out=small[0:1, 0:64], in_=pb[1][0:1, 0:64]),
                  waits=[pe_ns])
        p.v(lambda e: e.tensor_add(out=small[0:1, 0:32], in0=small[0:1, 0:32],
                                   in1=small[0:1, 32:64]))
        p.v(lambda e: e.tensor_add(out=small[0:1, 0:16], in0=small[0:1, 0:16],
                                   in1=small[0:1, 16:32]))
        p.v(lambda e: e.tensor_add(out=small[0:1, 0:8], in0=small[0:1, 0:8],
                                   in1=small[0:1, 8:16]))
        ev_sm = p.v(lambda e: e.tensor_add(out=small[0:1, 44:48], in0=small[0:1, 0:4],
                                           in1=small[0:1, 4:8]))
        pe_tc = p.mm(pb[1][0:4, 64:65], small[0:1, 44:48], one_f[:], True, True,
                     waits=[ev_sm], inc=s_pe)
        ev_tm = p.v(lambda e: e.tensor_scalar(
            out=small[0:4, 40:41], in0=pb[1][0:4, 64:65], scalar1=EPS_A,
            scalar2=NEG_BIG, op0=ALU.is_lt, op1=ALU.mult), waits=[pe_tc])
        ev_rel2 = p.v(lambda e: e.tensor_scalar(
            out=small[0:4, 16:32], in0=small[0:4, 16:32],
            scalar1=SCALE / 16384.0, scalar2=small[0:4, 40:41],
            op0=ALU.mult, op1=ALU.add), waits=[ev_ra, ev_tm])
        pe_rt = p.tp(pb[1][0:16, 0:4], small[0:4, 16:32], identf[0:4, 0:4],
                     waits=[ev_rel2], inc=s_pe)
        ev_w0 = p.v(lambda e: e.tensor_copy(out=relw[:, 0:4], in_=pb[1][0:16, 0:4]),
                    waits=[pe_rt])
        p.v(lambda e: e.reduce_max(out=relw[:, 4:5], in_=relw[:, 0:4], axis=AX.X))
        ev_nm = p.v(lambda e: e.tensor_scalar(
            out=relw[:, 5:6], in0=relw[:, 4:5], scalar1=-1.0, scalar2=None,
            op0=ALU.mult))
        ac_ex = p.a(lambda e: e.activation(out=relw[:, 8:12], in_=relw[:, 0:4],
                                           func=AF.Exp, bias=relw[:, 5:6]),
                    waits=[ev_nm])
        p.v(lambda e: e.reduce_sum(out=relw[:, 6:7], in_=relw[:, 8:12], axis=AX.X),
            waits=[ac_ex])
        p.v(lambda e: e.reciprocal(out=relw[:, 7:8], in_=relw[:, 6:7]))
        ev_wt = p.v(lambda e: e.tensor_scalar(
            out=relw[:, 8:12], in0=relw[:, 8:12], scalar1=relw[:, 7:8],
            scalar2=None, op0=ALU.mult))
        ac_g = p.a(lambda e: e.activation(out=small[0:1, 64:80], in_=gate_row[:],
                                          func=AF.Sigmoid), waits=cp_bar)
        ev_og = p.v(lambda e: e.tensor_scalar(
            out=small[0:1, 80:96], in0=small[0:1, 64:80], scalar1=-1.0, scalar2=1.0,
            op0=ALU.mult, op1=ALU.add), waits=[ac_g])
        pe_gc = p.mm(pb[1][0:16, 8:9], small[0:1, 64:80], one_f[:], True, True,
                     waits=[ac_g], inc=s_pe)
        ev_gc = p.v(lambda e: e.tensor_copy(out=relw[:, 12:13], in_=pb[1][0:16, 8:9]),
                    waits=[pe_gc])
        ev_At = p.v(lambda e: e.tensor_scalar(
            out=relw[:, 8:12], in0=relw[:, 8:12], scalar1=relw[:, 12:13],
            scalar2=None, op0=ALU.mult), waits=[ev_wt, ev_gc])
        for h in range(H):
            pe_A = p.mm(pb[2][:, 16 + h * 4:16 + (h + 1) * 4],
                        E16_sb[0:16, h * 128:(h + 1) * 128], relw[:, 8:12],
                        h == 0, False, waits=([ev_At, ld_e16] if h == 0 else None))
        pe_A = p.mm(pb[2][:, 0:16], ones_f[:], small[0:1, 80:96], False, True,
                    waits=[ev_og], inc=s_pe)
        ev_A = p.v(lambda e: e.tensor_copy(out=BG[:], in_=pb[2][:, 0:16]),
                   waits=[pe_A])
        for h in range(H):
            for r4 in range(4):
                ev_A = p.v(lambda e, h=h, r4=r4: e.tensor_copy(
                    out=A_rep[:, h * 16 + r4 * 4:h * 16 + (r4 + 1) * 4],
                    in_=pb[2][:, 16 + h * 4:16 + (h + 1) * 4]))
        cp_done = [ev_A, (s_vf, p.count(s_vf)), (s_ac, p.count(s_ac))]
        if "sbase" in dbg:
            p.dma("sync", dbg["sbase"].ap(), kvz_base, s_st,
                  waits=[(s_vf, p.count(s_vf))])
        if "wts" in dbg:
            p.dma("sync", dbg["wts"].ap(), relw[:, 8:12], s_st,
                  waits=[(s_vf, p.count(s_vf))])
        if "arep" in dbg:
            p.dma("sync", dbg["arep"].ap(), A_rep[:], s_st,
                  waits=[(s_vf, p.count(s_vf))])
        if "bg" in dbg:
            p.dma("sync", dbg["bg"].ap(), BG[:], s_st,
                  waits=[(s_vf, p.count(s_vf))])

        qkv_sb = [carve(AR, 24832, (8, 2052), BF16),
                  carve(AR, 57664, (8, 2052), BF16)]
        mem_rhs = carve(AR, 90496, (8192,), BF16)
        acc = carve(AR, 106880, (512,), F32)
        am23 = carve(AR, 108928, (1024,), BF16)
        attn_bf = carve(AR, 110976, (512,), BF16)
        attn_st = carve(AR, 112000, (512,), BF16)
        p2s = carve(AR, 113024, (64,), F32)
        sT_bf_t = carve(AR, 113280, (128,), BF16)

        ld_mem = p.dma("gpsimd", mem_rhs.rearrange("p (m h e) -> p m h e", m=M, h=H),
                       d_mem.ap().rearrange("m h d e -> d m h e"), s_ldg,
                       waits=cp_bar)

        Spsum = pb[0]
        numb = pb[1]
        denb = pb[2]
        sTb = pb[3]
        amb = [pb[4], pb[5], pb[6], pbf.ap().bitcast(F32)]
        done_qkv = [None, None]
        pe_done_num = None
        evac_done = cp_done
        st_attn = None
        for h in range(H):
            hp = h % 2
            ldq = p.dma("sync", qkv_sb[hp].rearrange("p g c -> p g c"),
                        d_qkv.ap()[h].rearrange("g p c -> p g c"), s_ld,
                        waits=([done_qkv[hp]] if done_qkv[hp] else
                               [(s_st, p.count(s_st))]))
            pe_init = p.mm(Spsum[:, 0:129], identf[:],
                           kvz_base[:, h * 129:(h + 1) * 129], True, False,
                           waits=[ev0m, (s_vf, p.count(s_vf))] if h == 0 else
                                 [(s_ac, p.count(s_ac))], inc=s_pe)
            pe_kv = pe_init
            ac_cast = None
            ev_mask = None
            last_kv_stop = None
            for g4 in range(8):
                pe_sT = None
                for ci in range(4):
                    n = g4 * 4 + ci
                    qp_v = qkv_sb[hp][:, g4, ci * 128:(ci + 1) * 128]
                    kpf_v = qkv_sb[hp][:, g4, 512 + ci * 128:512 + (ci + 1) * 128]
                    kpt_v = qkv_sb[hp][:, g4, 1024 + ci * 128:1024 + (ci + 1) * 128]
                    va_v = qkv_sb[hp][:, g4, 1536 + ci * 129:1536 + (ci + 1) * 129]
                    v_v = qkv_sb[hp][:, g4, 1536 + ci * 129:1536 + ci * 129 + 128]
                    ac_cast = p.a(lambda e, nb=n % 2: e.copy(
                        out=S_bf[nb][:], in_=Spsum[:, 0:129]),
                        waits=[pe_kv, ldq])
                    if "spsum" in dbg and h == 0 and g4 == 0 and ci == 2:
                        ev_dump = p.v(lambda e: e.tensor_copy(
                            out=kvz_tmp[:, 0:129], in_=Spsum[:, 0:129]),
                            waits=[pe_kv])
                        p.dma("sync", dbg["spsum"].ap(), kvz_tmp[:, 0:129], s_st,
                              waits=[ev_dump])
                    if "sbf0" in dbg and h == 0 and g4 == 0 and ci == 2:
                        p.dma("gpsimd", dbg["sbf0"].ap(), S_bf[0][:], s_st,
                              waits=[ac_cast])
                    wnum = [ac_cast] + ([pe_done_num] if (ci == 0 and pe_done_num)
                                        else [])
                    p.mm(numb[:, ci * 128:(ci + 1) * 128], qp_v,
                         S_bf[n % 2][:, 0:128], ci == 0, False, waits=wnum)
                    p.mm(denb[:, ci:ci + 1], qp_v, S_bf[n % 2][:, 128:129],
                         ci == 0, False)
                    p.mm(denb[:, 8 + ci * 4:12 + ci * 4], qp_v,
                         norms_bf[:, h * 4:(h + 1) * 4], False, False,
                         waits=([ldg_norm] if h == 0 and g4 == 0 and ci == 0
                                else None))
                    for m in range(M):
                        p.mm(amb[m][:, ci * 128:(ci + 1) * 128], qp_v,
                             mem_rhs[:, ((m * H) + h) * 128:((m * H) + h + 1) * 128],
                             ci == 0, False,
                             waits=([ld_mem] if h == 0 and g4 == 0 and ci == 0
                                    and m == 0 else None))
                    pe_sT = p.mm(sTb[:, 0:128], kpf_v, qp_v, True, True,
                                 waits=([ev_mask] if ev_mask else None), inc=s_pe)
                    ev_mask = p.v(lambda e: e.tensor_tensor(
                        out=attn_st[:, 0:128] if False else sT_bf_t,
                        in0=sTb[:, 0:128], in1=trimask[:], op=ALU.mult),
                        waits=[pe_sT])
                    p.mm(numb[:, ci * 128:(ci + 1) * 128], sT_bf_t, v_v,
                         False, False, waits=[ev_mask])
                    p.mm(denb[:, ci:ci + 1], sT_bf_t, ones_bf[:], False, False)
                    pe_kv = p.mm(Spsum[:, 0:129], kpt_v, va_v, False,
                                 h == H - 1 and n == NCH - 1, inc=s_pe)
                gw_ = [pe_kv]
                ev_d = p.v(lambda e: e.tensor_scalar(
                    out=p2s[:, 0:4], in0=denb[:, 0:4], scalar1=EPS_A, scalar2=None,
                    op0=ALU.max), waits=gw_)
                p.v(lambda e: e.reciprocal(out=p2s[:, 4:8], in_=p2s[:, 0:4]))
                ev_rl = p.v(lambda e, h=h: e.tensor_scalar(
                    out=p2s[:, 4:8], in0=p2s[:, 4:8], scalar1=BG[:, h:h + 1],
                    scalar2=None, op0=ALU.mult))
                ev_n1 = p.v(lambda e: e.tensor_scalar(
                    out=p2s[:, 16:32], in0=denb[:, 8:24], scalar1=EPS_A,
                    scalar2=None, op0=ALU.max))
                p.v(lambda e: e.reciprocal(out=p2s[:, 32:48], in_=p2s[:, 16:32]))
                ev_rn = p.v(lambda e, h=h: e.tensor_tensor(
                    out=p2s[:, 32:48], in0=p2s[:, 32:48],
                    in1=A_rep[:, h * 16:(h + 1) * 16], op=ALU.mult),
                    waits=(cp_done if h == 0 and g4 == 0 else None))
                po_m = None
                for ci in range(4):
                    ac_lo = p.a(lambda e, ci=ci: e.activation(
                        out=acc[:, ci * 128:(ci + 1) * 128],
                        in_=numb[:, ci * 128:(ci + 1) * 128], func=AF.Copy,
                        scale=p2s[:, 4 + ci:5 + ci]), waits=[ev_rl])
                    for m in range(4):
                        po_m = p.v(lambda e, ci=ci, m=m: e.scalar_tensor_tensor(
                            out=acc[:, ci * 128:(ci + 1) * 128],
                            in0=amb[m][:, ci * 128:(ci + 1) * 128],
                            scalar=p2s[:, 32 + ci * 4 + m:33 + ci * 4 + m],
                            in1=acc[:, ci * 128:(ci + 1) * 128],
                            op0=ALU.mult, op1=ALU.add),
                            waits=[ac_lo, ev_rn])
                if "den0" in dbg and h == 0 and g4 == 0:
                    p.dma("sync", dbg["den0"].ap(), p2s[:, 0:64], s_st, waits=[po_m])
                    p.dma("sync", dbg["acc0"].ap(), acc[:, 0:512], s_st, waits=[po_m])
                pe_done_num = po_m
                ac_cb = p.a(lambda e: e.copy(out=attn_bf[:, 0:512],
                                             in_=acc[:, 0:512]), waits=[po_m])
                for ci in range(4):
                    pe_t2 = p.tp(pbf[:, 512 + ci * 128:512 + (ci + 1) * 128],
                                 attn_bf[:, ci * 128:(ci + 1) * 128], identb[:],
                                 waits=([ac_cb, po_m] if ci == 0 else None),
                                 inc=(s_pe if ci == 3 else None))
                ac_t2 = p.a(lambda e: e.copy(out=attn_st[:, 0:512],
                                             in_=pbf[:, 512:1024]),
                            waits=[pe_t2] + ([st_attn] if st_attn else []))
                st_attn = p.dma("sync",
                                d_attn.ap()[h][:, g4 * 512:(g4 + 1) * 512],
                                attn_st[:, 0:512], s_st, waits=[ac_t2])
            done_qkv[hp] = pe_kv

        if "attn" in dbg:
            p.dma("gpsimd", dbg["attn"].ap(), d_attn.ap(), s_st,
                  waits=[(s_st, p.count(s_st))])

        p4_bar = [(s_st, p.count(s_st)), (s_pe, p.count(s_pe)),
                  (s_vf, p.count(s_vf)), (s_ac, p.count(s_ac)),
                  (s_pf, p.count(s_pf))]
        wo_bf = carve(AR, 0, (16, 2048), BF16)
        att4 = [carve(AR, 65536, (16, 128), BF16), carve(AR, 69632, (16, 128), BF16)]
        xc4 = [carve(AR, 73728, (2048,), F32), carve(AR, 81920, (2048,), F32)]
        yc4 = [carve(AR, 90112, (2048,), BF16), carve(AR, 94208, (2048,), BF16)]
        oc4 = [carve(AR, 98304, (2048,), F32), carve(AR, 106496, (2048,), F32)]

        ld_wo = p.dma("gpsimd", wo_bf,
                      d_wo.ap().rearrange("(k p) c -> p k c", p=128), s_ldg,
                      waits=p4_bar)
        done_a4 = [None, None]
        done_o4 = [None, None]
        last_out = None
        pey4 = [None] * 8
        done_p4pb = [None] * 8
        for c in range(NCH):
            bp = c % 2
            lda = p.dma("sync", att4[bp],
                        d_attn.ap()[:, :, c * 128:(c + 1) * 128]
                        .rearrange("h p c -> p h c"), s_ld,
                        waits=([done_a4[bp]] if done_a4[bp] else p4_bar))
            ldx = p.dma("sync", xc4[bp], d_x.ap()[c * 128:(c + 1) * 128, :], s_ld)
            ldy = p.dma("sync", yc4[bp], d_y.ap()[c * 128:(c + 1) * 128, :], s_ld,
                        waits=([y_done[c]]))
            bs = 3 * (c % 2)
            for o in range(4):
                for k in range(ND):
                    w0_ = None
                    if k == 0:
                        w0_ = [lda, ld_wo] +                               ([done_p4pb[bs + o]] if done_p4pb[bs + o] else [])
                    pey4[o] = p.mm(pb[bs + o][:, 0:512], att4[bp][:, k, :],
                                   wo_bf[:, k, o * 512:(o + 1) * 512],
                                   k == 0, k == ND - 1, waits=w0_,
                                   inc=(s_pe if k == ND - 1 else None))
            wo_done = pey4[3]
            for o in range(4):
                ev_s1 = p.v(lambda e, bs=bs, o=o, bp=bp: e.tensor_tensor(
                    out=oc4[bp][:, o * 512:(o + 1) * 512], in0=pb[bs + o][:, 0:512],
                    in1=xc4[bp][:, o * 512:(o + 1) * 512], op=ALU.add),
                    waits=[pey4[o], ldx] + ([done_o4[bp]] if (o == 0 and done_o4[bp])
                                            else []))
                done_p4pb[bs + o] = ev_s1
                po_s2 = p.g(lambda e, o=o, bp=bp: e.tensor_tensor(
                    out=oc4[bp][:, o * 512:(o + 1) * 512],
                    in0=oc4[bp][:, o * 512:(o + 1) * 512],
                    in1=yc4[bp][:, o * 512:(o + 1) * 512], op=ALU.add),
                    waits=[ev_s1, ldy])
            last_out = p.dma("sync", d_out.ap()[c * 128:(c + 1) * 128, :],
                             oc4[bp], s_out, waits=[po_s2])
            done_o4[bp] = last_out
            done_a4[bp] = wo_done
        p.op("gpsimd", lambda e: e.nop(), waits=[last_out], nochain=True)

        block = ctx.enter_context(nc.Block())
        p.emit(block)
    return nc


def _host_prep(inputs):
    x = np.ascontiguousarray(inputs["L0_attn_hidden_states"].reshape(B * S, D))
    shared = {
        "wq_t": np.ascontiguousarray(inputs["L0_attn_w_q"].T),
        "wk_t": np.ascontiguousarray(inputs["L0_attn_w_k"].T),
        "wv_t": np.ascontiguousarray(inputs["L0_attn_w_v"].T),
        "wo_t": np.ascontiguousarray(inputs["L0_attn_w_o"].T),
        "memories": np.ascontiguousarray(inputs["L0_attn_memories"]),
        "memnorms_t": np.ascontiguousarray(
            inputs["L0_attn_memory_norms"].transpose(2, 1, 0).reshape(HD, H * M)),
        "gate_row": np.ascontiguousarray(inputs["L0_attn_gate"].reshape(1, H)),
        "gate_wT": np.ascontiguousarray(inputs["L0_ffn_gate_w"].T),
        "ebias_row": np.ascontiguousarray(inputs["L0_ffn_expert_bias"].reshape(1, E)),
        "w1t": np.ascontiguousarray(inputs["L0_ffn_w1"].transpose(0, 2, 1)),
        "w3t": np.ascontiguousarray(inputs["L0_ffn_w3"].transpose(0, 2, 1)),
        "w2t": np.ascontiguousarray(inputs["L0_ffn_w2"].transpose(0, 2, 1)),
        "sw1t": np.ascontiguousarray(inputs["L0_ffn_sw1"].T),
        "sw3t": np.ascontiguousarray(inputs["L0_ffn_sw3"].T),
        "sw2t": np.ascontiguousarray(inputs["L0_ffn_sw2"].T),
        "identb": np.eye(128, dtype=BF),
        "identf": np.eye(128, dtype=np.float32),
        "trimask": np.triu(np.ones((128, 128), np.float32)).astype(BF),
        "E16": np.repeat(np.eye(16, dtype=np.float32)[:, :, None], 128, axis=2)
              .reshape(16, 16 * 128),
        "E8": np.repeat(np.eye(8, dtype=np.float32)[:, :, None], 128, axis=2)
              .reshape(8, 8 * 128).astype(BF),
    }
    in_maps = []
    for c in range(NCORES):
        b, q = divmod(c, 4)
        sl = x[c * T:(c + 1) * T]
        pref = np.zeros(8, np.float32)
        rl = np.zeros(8, np.float32)
        for qq in range(4):
            rl[b * 4 + qq] = 1.0
            if qq < q:
                pref[b * 4 + qq] = 1.0
        m = dict(shared)
        m["x_tm"] = sl
        m["x_fm"] = np.ascontiguousarray(sl.T)
        m["prefmask"] = pref.reshape(1, 8)
        m["relmask"] = rl.reshape(1, 8)
        in_maps.append(m)
    return in_maps


def kernel(**inputs):
    in_maps = _host_prep(inputs)
    nc = build()
    res = run_bass_kernel_spmd(nc, in_maps, core_ids=list(range(NCORES)))
    out = np.concatenate([r["out"] for r in res.results], axis=0)
    return out.reshape(B, S, D)


def run_debug(inputs, cores=(0,)):
    in_maps = _host_prep(inputs)
    nc = build()
    res = run_bass_kernel_spmd(nc, [in_maps[c] for c in cores], core_ids=list(cores))
    return res.results


# version 09
# speedup vs baseline: 1.1830x; 1.1830x over previous
import numpy as np
import ml_dtypes
from contextlib import ExitStack

import concourse.bass as bass
import concourse.mybir as mybir
from concourse.bass_utils import run_bass_kernel_spmd

F32 = mybir.dt.float32
BF16 = mybir.dt.bfloat16
I32 = mybir.dt.int32
AF = mybir.ActivationFunctionType
ALU = mybir.AluOpType
AX = mybir.AxisListType
BF = ml_dtypes.bfloat16

NCORES = 8
B, S, D = 2, 16384, 2048
T = 4096
NCH = 32
ND = 16
H, HD = 16, 128
M = 4
E = 8
F, NF = 1408, 11
SCALE = HD ** -0.5
LN_EPS = 1e-5
EPS_A = 1e-6
NEG_BIG = -1e30

DEBUG = ()
P3_PASSES = 9


class Prog:
    ENGINES = ("sync", "scalar", "vector", "tensor", "gpsimd")

    CHAIN = ("vector", "scalar", "gpsimd")

    def __init__(self, nc, chain_sems=None):
        self.nc = nc
        self.ops = {k: [] for k in self.ENGINES}
        self.cnt = {}
        self.chain_sem = chain_sems or {}

    def count(self, sem):
        return self.cnt.get(sem.name, 0)

    def op(self, eng, fn, waits=None, inc=None, inc16=None, nochain=False):
        waits = list(waits or [])
        chained = eng in self.CHAIN and self.chain_sem[eng] is not None
        if chained:
            cs = self.chain_sem[eng]
            waits.append((cs, self.count(cs)))
        w = []
        for w_ in waits:
            if w_ is None:
                continue
            (sem, val) = w_
            assert val <= self.count(sem), \
                f"wait on future count {sem.name} {val} > {self.count(sem)}"
            if val > 0:
                w.append((sem, val))
        incs = []
        assert inc is None or inc16 is None
        assert not (chained and not nochain and (inc is not None or inc16 is not None))
        mile = None
        if inc is not None:
            self.cnt[inc.name] = self.count(inc) + 1
            incs.append((inc, 1))
            mile = (inc, self.count(inc))
        elif inc16 is not None:
            self.cnt[inc16.name] = self.count(inc16) + 16
            incs.append((inc16, 16))
            mile = (inc16, self.count(inc16))
        elif chained and not nochain:
            cs = self.chain_sem[eng]
            self.cnt[cs.name] = self.count(cs) + 1
            incs.append((cs, 1))
            mile = (cs, self.count(cs))
        self.ops[eng].append((fn, w, incs))
        return mile

    def dma(self, q, out, in_, sem, waits=None, **kw):
        return self.op(q, (lambda e, o=out, i=in_, kw=kw:
                           e.dma_start(out=o, in_=i, **kw)),
                       waits=waits, inc16=sem, nochain=True)

    def mm(self, out, lhsT, rhs, start, stop, waits=None, inc=None):
        return self.op("tensor",
                       (lambda e, o=out, l=lhsT, r=rhs, st=start, sp=stop:
                        e.matmul(o, l, r, start=st, stop=sp)),
                       waits=waits, inc=inc)

    def tp(self, out, in_, ident, waits=None, inc=None):
        return self.op("tensor",
                       (lambda e, o=out, i=in_, d=ident: e.transpose(o, i, d)),
                       waits=waits, inc=inc)

    def v(self, fn, waits=None, inc=None):
        return self.op("vector", fn, waits=waits, inc=inc)

    def a(self, fn, waits=None, inc=None):
        return self.op("scalar", fn, waits=waits, inc=inc)

    def g(self, fn, waits=None, inc=None, nochain=False):
        return self.op("gpsimd", fn, waits=waits, inc=inc, nochain=nochain)

    def emit(self, block):
        for eng_name in self.ENGINES:
            ops = self.ops[eng_name]

            def body(eng, ops=ops):
                hwm = {}
                for (fn, waits, incs) in ops:
                    for (sem, val) in waits:
                        if val > hwm.get(sem.name, 0):
                            eng.wait_ge(sem, val)
                            hwm[sem.name] = val
                    inst = fn(eng)
                    for (sem, val) in incs:
                        inst = inst.then_inc(sem, val)

            getattr(block, eng_name)(body)


def build():
    nc = bass.Bass("TRN2", target_bir_lowering=False)

    d_x = nc.dram_tensor("x_tm", [T, D], F32, kind="ExternalInput")
    d_xfm = nc.dram_tensor("x_fm", [D, T], F32, kind="ExternalInput")
    d_wq = nc.dram_tensor("wq_t", [D, D], F32, kind="ExternalInput")
    d_wk = nc.dram_tensor("wk_t", [D, D], F32, kind="ExternalInput")
    d_wv = nc.dram_tensor("wv_t", [D, D], F32, kind="ExternalInput")
    d_wo = nc.dram_tensor("wo_t", [D, D], F32, kind="ExternalInput")
    d_mem = nc.dram_tensor("memories", [M, H, HD, HD], F32, kind="ExternalInput")
    d_mnorm = nc.dram_tensor("memnorms_t", [HD, H * M], F32, kind="ExternalInput")
    d_gate = nc.dram_tensor("gate_row", [1, H], F32, kind="ExternalInput")
    d_gw = nc.dram_tensor("gate_wT", [D, E], F32, kind="ExternalInput")
    d_ebias = nc.dram_tensor("ebias_row", [1, E], F32, kind="ExternalInput")
    d_w1 = nc.dram_tensor("w1t", [E, D, F], F32, kind="ExternalInput")
    d_w3 = nc.dram_tensor("w3t", [E, D, F], F32, kind="ExternalInput")
    d_w2 = nc.dram_tensor("w2t", [E, F, D], F32, kind="ExternalInput")
    d_sw1 = nc.dram_tensor("sw1t", [D, F], F32, kind="ExternalInput")
    d_sw3 = nc.dram_tensor("sw3t", [D, F], F32, kind="ExternalInput")
    d_sw2 = nc.dram_tensor("sw2t", [F, D], F32, kind="ExternalInput")
    d_identb = nc.dram_tensor("identb", [128, 128], BF16, kind="ExternalInput")
    d_identf = nc.dram_tensor("identf", [128, 128], F32, kind="ExternalInput")
    d_trimask = nc.dram_tensor("trimask", [128, 128], BF16, kind="ExternalInput")
    d_E16 = nc.dram_tensor("E16", [16, 16 * 128], F32, kind="ExternalInput")
    d_E8 = nc.dram_tensor("E8", [8, 8 * 128], BF16, kind="ExternalInput")
    d_prefmask = nc.dram_tensor("prefmask", [1, 8], F32, kind="ExternalInput")
    d_relmask = nc.dram_tensor("relmask", [1, 8], F32, kind="ExternalInput")

    d_xnfm = nc.dram_tensor("xnfm", [ND, 128, T], BF16)
    d_qkv = nc.dram_tensor("qkv", [H, 8, 128, 2052], BF16)
    d_attn = nc.dram_tensor("attnfm", [H, 128, T], BF16)
    d_y = nc.dram_tensor("ydram", [T, D], BF16)
    d_ccin = nc.dram_tensor("ccin", [128, 2080], F32)
    d_ccout = nc.dram_tensor("ccout", [8 * 128, 2080], F32)

    d_out = nc.dram_tensor("out", [T, D], F32, kind="ExternalOutput")
    dbg = {}
    for name, shape in (("xn", [T, D]), ("scores", [T, E]), ("alphas", [T, E]),
                        ("yffn", [T, D]), ("hsb", [128, 11 * 2048]), ("xin0", [128, 16 * 2048]),
                        ("attn", [H, 128, T]), ("den0", [128, 64]),
                        ("acc0", [128, 512]), ("sbf0", [128, 129]), ("spsum", [128, 129]),
                        ("vals8", [T, E]), ("ind", [T, E]),
                        ("xnsum", [128, ND]), ("alphasT", [8, T]),
                        ("qkv0", [8, 128, 2052]), ("kvz", [128, 2080]),
                        ("sbase", [128, 2064]), ("wts", [16, 4]),
                        ("arep", [128, 256]), ("bg", [128, 16])):
        if name in DEBUG:
            dbg[name] = nc.dram_tensor("dbg_" + name, shape, F32, kind="ExternalOutput")

    with ExitStack() as ctx:
        def sbuf(name, shape, dt=F32):
            return ctx.enter_context(nc.sbuf_tensor("sb_" + name, shape, dt))

        def psum(name, shape, dt=F32):
            return ctx.enter_context(nc.psum_tensor(name, shape, dt))

        def sem(name):
            return ctx.enter_context(nc.semaphore(name))

        identb = sbuf("identb", [128, 128], BF16)
        identf = sbuf("identf", [128, 128], F32)
        trimask = sbuf("trimask", [128, 128], BF16)
        ones_f = sbuf("ones_f", [1, 128], F32)
        ones_c = sbuf("ones_c", [128, 1], F32)
        ones_bf = sbuf("ones_bf", [128, 1], BF16)
        one_f = sbuf("one_f", [1, 1], F32)
        gate_row = sbuf("gate_row_sb", [1, H], F32)
        ebias_row = sbuf("ebias_sb", [1, E], F32)
        gw_sb = sbuf("gw_sb", [128, ND * E], F32)
        bias_bc = sbuf("bias_bc", [128, E], F32)
        cs_bc = sbuf("cs_bc", [128, E], F32)
        alphas = sbuf("alphas", [128, NCH * E], F32)
        alphasT = sbuf("alphasT", [8, T], BF16)
        xnsum = sbuf("xnsum", [128, ND], F32)
        xnsum_bf = sbuf("xnsum_bf", [128, ND], BF16)
        qsum_bf = sbuf("qsum_bf", [128, H], BF16)
        norms_bf = sbuf("norms_bf", [128, H * M], BF16)
        A_rep = sbuf("A_rep", [128, H * 16], F32)
        BG = sbuf("BG", [128, H], F32)
        S_bf = [sbuf("S_bf0", [128, 129], BF16), sbuf("S_bf1", [128, 129], BF16)]
        E8_sb = sbuf("E8_sb", [8, 8 * 128], BF16)
        prefm = sbuf("prefm", [1, 8], F32)
        relm = sbuf("relm", [1, 8], F32)
        small = sbuf("small", [16, 128], F32)
        relw = sbuf("relw", [16, 16], F32)

        AR = sbuf("AR", [128, 181 * 1024], mybir.dt.uint8)
        AR_M = ("seg", 0)
        AR_A = ("seg", 34816)
        AR_W = ("seg", 108544)

        def carve(arena, off_bytes, shape, dt):
            if isinstance(arena, tuple):
                off_bytes += arena[1]
            esz = {F32: 4, BF16: 2, I32: 4}[dt]
            assert off_bytes % esz == 0
            ap = AR.ap().bitcast(dt)
            n = int(np.prod(shape))
            v = ap[:, off_bytes // esz: off_bytes // esz + n]
            if len(shape) > 1:
                pat = " ".join(f"s{i}" for i in range(len(shape)))
                v = v.rearrange(f"p ({pat}) -> p {pat}",
                                **{f"s{i}": int(shape[i]) for i in range(len(shape))})
            return v

        pb = [psum(f"pb{i}", [128, 512], F32) for i in range(7)]
        pbf = psum("pbbf", [128, 1024], BF16)

        s_ld = sem("s_ld")
        s_ldg = sem("s_ldg")
        s_st = sem("s_st")
        s_stg = sem("s_stg")
        s_pe = sem("s_pe")
        s_ev = sem("s_ev")
        s_ac = sem("s_ac")
        s_po = sem("s_po")
        s_cc = sem("s_cc")
        s_out = sem("s_out")
        s_vf = sem("s_vf")
        s_pf = sem("s_pf")

        p = Prog(nc, chain_sems={"vector": s_vf, "scalar": s_ac,
                                 "gpsimd": s_pf})

        lds = []
        for dst, src in ((identb, d_identb), (identf, d_identf), (trimask, d_trimask),
                         (gate_row, d_gate), (ebias_row, d_ebias),
                         (prefm, d_prefmask), (relm, d_relmask)):
            lds.append(p.dma("sync", dst[:], src.ap(), s_ld))
        p.dma("sync", E8_sb[:], d_E8.ap(), s_ld)
        ld0 = p.dma("sync", gw_sb[:].rearrange("p (k e) -> p k e", e=E),
                    d_gw.ap().rearrange("(k p) e -> p k e", p=128), s_ld)
        ldg_norm = p.dma("gpsimd", norms_bf[:], d_mnorm.ap(), s_ldg)

        p.v(lambda e: e.memset(ones_f[:], 1.0))
        p.v(lambda e: e.memset(ones_c[:], 1.0))
        p.v(lambda e: e.memset(ones_bf[:], 1.0))
        p.v(lambda e: e.memset(one_f[:], 1.0))
        ev0 = p.v(lambda e: e.memset(xnsum[:], 0.0))

        p.mm(pb[0][:, 0:E], ones_f[:], ebias_row[:], True, True,
             waits=[ld0, ev0])
        for k in range(ND):
            pe_cs = p.mm(pb[1][0:1, 0:E], ones_c[:, 0:1], gw_sb[:, k * E:(k + 1) * E],
                         k == 0, k == ND - 1, inc=(s_pe if k == ND - 1 else None))
        p.v(lambda e: e.tensor_copy(out=bias_bc[:], in_=pb[0][:, 0:E]),
            waits=[pe_cs])
        ev_b = p.v(lambda e: e.tensor_copy(out=small[0:1, 0:E], in_=pb[1][0:1, 0:E]))
        pe_cb = p.mm(pb[0][:, 8:16], ones_f[:], small[0:1, 0:E], True, True,
                     waits=[ev_b], inc=s_pe)
        ev_cs = p.v(lambda e: e.tensor_copy(out=cs_bc[:], in_=pb[0][:, 8:16]),
                    waits=[pe_cb])

        xch = [carve(AR_M, 0, (2048,), F32), carve(AR_M, 8192, (2048,), F32)]
        xfmch = [carve(AR_M, 16384, (16, 128), F32), carve(AR_M, 24576, (16, 128), F32)]
        sqdump = carve(AR_A, 0, (2048,), F32)
        xnbf = carve(AR_A, 8192, (2048,), BF16)
        xnfmc = [carve(AR_A, 12288, (16, 128), BF16), carve(AR_A, 16384, (16, 128), BF16)]
        stat = carve(AR_A, 20480, (16,), F32)
        sc8 = carve(AR_A, 20544, (48,), F32)
        albf = carve(AR_A, 20736, (8,), BF16)
        xns_t = carve(AR_A, 20752, (16,), F32)
        dbgbuf = carve(AR_A, 24576, (2048,), F32)

        done_xc = [None, None]
        done_xf = [None, None]
        done_xnfm = [None, None]
        done_xnbf = None
        done_albf = None
        for c in range(NCH):
            bp = c % 2
            p.dma("sync", xch[bp], d_x.ap()[c * 128:(c + 1) * 128, :], s_ld,
                  waits=([done_xc[bp]] if done_xc[bp] else []))
            ldc = p.dma("sync", xfmch[bp],
                        d_xfm.ap()[:, c * 128:(c + 1) * 128]
                        .rearrange("(dt p) c -> p dt c", p=128), s_ld,
                        waits=([done_xf[bp]] if done_xf[bp] else []))

            xc = xch[bp]
            xf = xfmch[bp]
            mu = stat[:, 0:1]; rstd = stat[:, 1:2]; nrstd = stat[:, 2:3]
            ssq = stat[:, 3:4]; t0 = stat[:, 4:5]
            p.v(lambda e, xc=xc, t0=t0: e.reduce_sum(out=t0, in_=xc, axis=AX.X),
                waits=[ldc, ev_cs])
            p.v(lambda e, t0=t0, mu=mu: e.tensor_scalar(
                out=mu, in0=t0, scalar1=1.0 / D, scalar2=None, op0=ALU.mult))
            ev_xc = p.v(lambda e, xc=xc, mu=mu: e.tensor_scalar(
                out=xc, in0=xc, scalar1=mu, scalar2=None, op0=ALU.subtract))
            ac_sq = p.a(lambda e, xc=xc, ssq=ssq: e.activation(
                out=sqdump, in_=xc, func=AF.Square, accum_out=ssq),
                waits=[ev_xc])
            p.v(lambda e, ssq=ssq, t0=t0: e.tensor_scalar(
                out=t0, in0=ssq, scalar1=1.0 / D, scalar2=LN_EPS,
                op0=ALU.mult, op1=ALU.add), waits=[ac_sq])
            ev_t0 = p.v(lambda e: e.memset(small[0:1, 32:33], 0.0))
            ac_rt = p.a(lambda e, t0=t0: e.sqrt(out=t0, in_=t0), waits=[ev_t0])
            p.v(lambda e, t0=t0, rstd=rstd: e.reciprocal(out=rstd, in_=t0),
                waits=[ac_rt])
            p.v(lambda e, rstd=rstd, nrstd=nrstd: e.tensor_scalar(
                out=nrstd, in0=rstd, scalar1=-1.0, scalar2=None, op0=ALU.mult))
            ev_xn = p.v(lambda e, xc=xc, rstd=rstd: e.tensor_scalar(
                out=xnbf, in0=xc, scalar1=rstd, scalar2=None, op0=ALU.mult),
                waits=([done_xnbf] if done_xnbf else []))
            done_xc[bp] = ev_xn

            for k in range(ND):
                pe_u = p.mm(pb[2][:, 0:E], xf[:, k, :], gw_sb[:, k * E:(k + 1) * E],
                            k == 0, k == ND - 1,
                            waits=([ldc, ev_cs] if k == 0 else None),
                            inc=(s_pe if k == ND - 1 else None))
            done_xf[bp] = pe_u
            vsel = sc8[:, 0:8]; sco = sc8[:, 8:16]; vals8 = sc8[:, 16:24]
            ind = sc8[:, 24:32]
            ev_t = p.v(lambda e, vsel=vsel, mu=mu: e.scalar_tensor_tensor(
                out=vsel, in0=cs_bc[:], scalar=mu, in1=pb[2][:, 0:E],
                op0=ALU.mult, op1=ALU.subtract), waits=[pe_u])
            ac_sg = p.a(lambda e, sco=sco, vsel=vsel, nrstd=nrstd: e.activation(
                out=sco, in_=vsel, func=AF.Sigmoid, scale=nrstd),
                waits=[ev_t])
            p.v(lambda e, vsel=vsel, sco=sco: e.tensor_add(
                out=vsel, in0=sco, in1=bias_bc[:]), waits=[ac_sg])
            p.v(lambda e, vals8=vals8, vsel=vsel: e.max(out=vals8, in_=vsel))
            p.v(lambda e, ind=ind, vsel=vsel, vals8=vals8: e.tensor_scalar(
                out=ind, in0=vsel, scalar1=vals8[:, 1:2], scalar2=None, op0=ALU.is_ge))
            ev_al = p.v(lambda e, ind=ind, sco=sco, c=c: e.tensor_mul(
                out=alphas[:, c * E:(c + 1) * E], in0=ind, in1=sco))

            ac_ab = p.a(lambda e, c=c: e.copy(
                out=albf, in_=alphas[:, c * E:(c + 1) * E]),
                waits=[ev_al] + ([done_albf] if done_albf else []))
            pe_at = p.tp(pbf[0:8, 0:128], albf, identb[:],
                         waits=[(s_ac, p.count(s_ac))], inc=s_pe)
            done_albf = pe_at
            ac_at = p.a(lambda e, c=c: e.copy(
                out=alphasT[:, c * 128:(c + 1) * 128], in_=pbf[0:8, 0:128]),
                waits=[pe_at])

            wspill = [done_xnfm[bp]] if done_xnfm[bp] else []
            for half in range(2):
                for t8 in range(8):
                    dt_i = half * 8 + t8
                    wts_ = None
                    if t8 == 0:
                        wts_ = ([ev_xn, (s_ac, p.count(s_ac))]
                                + (wspill if half == 0 else []))
                    pe_t = p.tp(pbf[:, t8 * 128:(t8 + 1) * 128],
                                xnbf[:, dt_i * 128:(dt_i + 1) * 128],
                                identb[:], waits=wts_,
                                inc=(s_pe if t8 == 7 else None))
                ac_e = p.a(lambda e, half=half, bp=bp: e.copy(
                    out=xnfmc[bp][:, half * 8:(half + 1) * 8, :]
                    .rearrange("p a b -> p (a b)"),
                    in_=pbf[:, 0:1024]), waits=[pe_t])
            done_xnbf = pe_t
            ac_t = (s_ac, p.count(s_ac))
            p.v(lambda e, bp=bp: e.tensor_reduce(
                out=xns_t, in_=xnfmc[bp], axis=AX.X, op=ALU.add),
                waits=[ac_t])
            p.v(lambda e: e.tensor_add(out=xnsum[:], in0=xnsum[:], in1=xns_t))
            st_x = p.dma("sync", d_xnfm.ap()[:, :, c * 128:(c + 1) * 128]
                         .rearrange("dt p c -> p dt c"),
                         xnfmc[bp], s_st, waits=[ac_t])
            done_xnfm[bp] = st_x

            if "xn" in dbg:
                ev_d = p.v(lambda e: e.tensor_copy(out=dbgbuf, in_=xnbf), waits=[ev_xn])
                p.dma("sync", dbg["xn"].ap()[c * 128:(c + 1) * 128, :], dbgbuf, s_st,
                      waits=[ev_d])
            if "scores" in dbg:
                p.dma("sync", dbg["scores"].ap()[c * 128:(c + 1) * 128, :], sco, s_st,
                      waits=[ev_al])
            if "vals8" in dbg:
                p.dma("sync", dbg["vals8"].ap()[c * 128:(c + 1) * 128, :], vals8, s_st,
                      waits=[ev_al])
                p.dma("sync", dbg["ind"].ap()[c * 128:(c + 1) * 128, :], ind, s_st,
                      waits=[ev_al])
            if "alphas" in dbg:
                p.dma("sync", dbg["alphas"].ap()[c * 128:(c + 1) * 128, :],
                      alphas[:, c * E:(c + 1) * E], s_st, waits=[ev_al])

        ev_last = p.v(lambda e: e.memset(small[0:1, 33:34], 0.0))
        ac_xs = p.a(lambda e: e.copy(out=xnsum_bf[:], in_=xnsum[:]),
                    waits=[ev_last])
        if "xnsum" in dbg:
            p.dma("sync", dbg["xnsum"].ap(), xnsum[:], s_st, waits=[ev_last])
        if "alphasT" in dbg:
            ev_d = p.v(lambda e: e.tensor_copy(
                out=carve(AR_M, 0, (4096,), F32)[0:8, :], in_=alphasT[:]),
                waits=[(s_ac, p.count(s_ac))])
            p.dma("sync", dbg["alphasT"].ap(), carve(AR_M, 0, (4096,), F32)[0:8, :],
                  s_st, waits=[ev_d])

        p1_bar = [done_xnfm[0], done_xnfm[1], done_xnbf, ac_xs]

        wqkv = carve(AR_W, 0, (3, 4, 16, 128), BF16)
        xng = [carve(AR_A, 0, (16, 512), BF16),
               carve(AR_A, 16384, (16, 512), BF16)]
        stage = [carve(AR_A, 32768, (2052,), BF16),
                 carve(AR_A, 36872, (2052,), BF16)]
        tmpA = carve(AR_A, 40976, (512,), F32)
        tmpB = carve(AR_A, 43024, (512,), F32)
        cc_sb = carve(AR_A, 45072, (2080,), F32)

        for i in range(2):
            for ci in range(4):
                p.v(lambda e, i=i, ci=ci: e.memset(
                    stage[i][:, 1536 + ci * 129 + 128: 1536 + ci * 129 + 129], 1.0),
                    waits=(p1_bar if (i == 0 and ci == 0) else None))

        done_xng = [None, None]
        done_stage = [None, None]
        done_w = [None, None, None]
        done_pb = {0: None, 1: None, 2: None}
        done_kvz = None
        ac_qs = None
        for hs in range(4):
            heads = [hs * 4 + i for i in range(4)]
            wld = None
            for mi, dw in enumerate((d_wq, d_wk, d_wv)):
                for hh in range(4):
                    h = heads[hh]
                    wld = p.dma("gpsimd", wqkv[:, mi, hh],
                                dw.ap()[:, h * 128:(h + 1) * 128]
                                .rearrange("(k p) c -> p k c", p=128),
                                s_ldg,
                                waits=((p1_bar if hs == 0 else []) +
                                       ([done_w[mi]] if done_w[mi] else [])
                                       if hh == 0 else None))
            for hh in range(4):
                for k in range(ND):
                    pe_qs = p.mm(pb[2][:, hh:hh + 1], wqkv[:, 0, hh, k],
                                 xnsum_bf[:, k:k + 1],
                                 hh == 0 and k == 0, hh == 3 and k == ND - 1,
                                 waits=([wld, ac_xs] +
                                        ([done_pb[2]] if done_pb[2] else [])
                                        if (hh == 0 and k == 0) else None),
                                 inc=(s_pe if (hh == 3 and k == ND - 1) else None))
            ac_qs = p.a(lambda e, hs=hs: e.copy(
                out=qsum_bf[:, hs * 4:hs * 4 + 4], in_=pb[2][:, 0:4]),
                waits=[pe_qs])
            done_pb[2] = ac_qs

            pe_kvg = None
            for g in range(8):
                bp = g % 2
                ldx = p.dma("sync", xng[bp],
                            d_xnfm.ap()[:, :, g * 512:(g + 1) * 512]
                            .rearrange("dt p c -> p dt c"), s_ld,
                            waits=([done_xng[bp]] if done_xng[bp]
                                   else (p1_bar if hs == 0 else [])))
                pe_last_v = None
                for hh in range(4):
                    h = heads[hh]
                    sp = (g * 4 + hh) % 2
                    wst = ([x for x in (done_stage[sp],) if x] or [])
                    for k in range(ND):
                        pe_q = p.mm(pb[0][:, 0:512], wqkv[:, 0, hh, k],
                                    xng[bp][:, k, :], k == 0, k == ND - 1,
                                    waits=(([ldx, wld] +
                                            ([done_pb[0]] if done_pb[0] else []))
                                           if k == 0 else None),
                                    inc=(s_pe if k == ND - 1 else None))
                    ev_m = p.v(lambda e: e.tensor_scalar(
                        out=tmpA, in0=pb[0][:, 0:512], scalar1=0.0, scalar2=None,
                        op0=ALU.min), waits=[pe_q] + wst)
                    ac_e = p.a(lambda e: e.activation(out=tmpB, in_=tmpA, func=AF.Exp),
                               waits=[ev_m])
                    ev_qp = p.v(lambda e, sp=sp: e.scalar_tensor_tensor(
                        out=stage[sp][:, 0:512], in0=pb[0][:, 0:512], scalar=0.0,
                        in1=tmpB, op0=ALU.max, op1=ALU.add), waits=[ac_e])
                    done_pb[0] = ev_qp
                    for k in range(ND):
                        pe_k = p.mm(pb[1][:, 0:512], wqkv[:, 1, hh, k],
                                    xng[bp][:, k, :], k == 0, k == ND - 1,
                                    waits=(([done_pb[1]] if done_pb[1] else [])
                                           if k == 0 else None),
                                    inc=(s_pe if k == ND - 1 else None))
                    ev_m = p.v(lambda e: e.tensor_scalar(
                        out=tmpA, in0=pb[1][:, 0:512], scalar1=0.0, scalar2=None,
                        op0=ALU.min), waits=[pe_k])
                    ac_e = p.a(lambda e: e.activation(out=tmpB, in_=tmpA, func=AF.Exp),
                               waits=[ev_m])
                    ev_kp = p.v(lambda e, sp=sp: e.scalar_tensor_tensor(
                        out=stage[sp][:, 512:1024], in0=pb[1][:, 0:512], scalar=0.0,
                        in1=tmpB, op0=ALU.max, op1=ALU.add), waits=[ac_e])
                    done_pb[1] = ev_kp
                    for ci in range(4):
                        pe_tp = p.tp(pbf[:, ci * 128:(ci + 1) * 128],
                                     stage[sp][:, 512 + ci * 128:512 + (ci + 1) * 128],
                                     identb[:],
                                     waits=([ev_kp, (s_ac, p.count(s_ac))]
                                            if ci == 0 else None),
                                     inc=(s_pe if ci == 3 else None))
                    ac_tp = p.a(lambda e, sp=sp: e.copy(
                        out=stage[sp][:, 1024:1536], in_=pbf[:, 0:512]),
                        waits=[pe_tp])
                    ac_v = None
                    for cpair in range(2):
                        for ci2 in range(2):
                            ci = cpair * 2 + ci2
                            for k in range(ND):
                                pe_v = p.mm(pb[2][:, ci2 * 128:(ci2 + 1) * 128],
                                            xng[bp][:, k, ci * 128:(ci + 1) * 128],
                                            wqkv[:, 2, hh, k],
                                            k == 0 and ci2 == 0, k == ND - 1,
                                            waits=(([done_pb[2]]
                                                    if (done_pb[2] and ci2 == 0)
                                                    else None) if k == 0 else None),
                                            inc=(s_pe if (k == ND - 1 and ci2 == 1)
                                                 else None))
                        ac_v = p.a(lambda e, sp=sp, cpair=cpair: e.copy(
                            out=stage[sp][:, 1536 + cpair * 258:
                                          1536 + (cpair + 1) * 258]
                            .rearrange("p (a b) -> p a b", b=129)[:, :, 0:128],
                            in_=pb[2][:, 0:256]
                            .rearrange("p (a b) -> p a b", b=128)), waits=[pe_v])
                        done_pb[2] = ac_v
                    last_q, last_k, last_v = pe_q, pe_k, pe_v
                    for ci in range(4):
                        pe_kvg = p.mm(pb[3 + hh][:, 0:129],
                                      stage[sp][:, 1024 + ci * 128:1024 + (ci + 1) * 128],
                                      stage[sp][:, 1536 + ci * 129:1536 + (ci + 1) * 129],
                                      g == 0 and ci == 0, g == 7 and ci == 3,
                                      waits=(([ac_tp, ac_v] +
                                              ([done_kvz] if (g == 0 and ci == 0)
                                               else []))
                                             if ci == 0 else None),
                                      inc=(s_pe if ci == 3 else None))
                    st_sp = p.dma("sync", d_qkv.ap()[h, g], stage[sp], s_st,
                                  waits=[ev_qp, ac_tp, ac_v, pe_kvg])
                    done_stage[sp] = st_sp
                    pe_last_v = pe_kvg
                done_xng[bp] = pe_last_v
            done_w = [last_q, last_k, last_v]
            ev_kvz = None
            for hh in range(4):
                ev_kvz = p.v(lambda e, hs=hs, hh=hh: e.tensor_copy(
                    out=cc_sb[:, (hs * 4 + hh) * 129:(hs * 4 + hh + 1) * 129],
                    in_=pb[3 + hh][:, 0:129]), waits=[pe_last_v])
            done_kvz = ev_kvz

        for h in range(H):
            pe_rel = p.mm(pb[0][0:4, h:h + 1], norms_bf[:, h * 4:(h + 1) * 4],
                          qsum_bf[:, h:h + 1], h == 0, h == H - 1,
                          waits=([ldg_norm, ac_qs, done_pb[0]] if h == 0 else None),
                          inc=(s_pe if h == H - 1 else None))
        ev_rel = p.v(lambda e: e.tensor_copy(
            out=cc_sb[0:4, 2064:2080], in_=pb[0][0:4, 0:16]), waits=[pe_rel])
        st_cc = p.dma("sync", d_ccin.ap(), cc_sb, s_st, waits=[ev_rel])
        cc_done = p.g(lambda e: e.collective_compute(
            "AllGather", ALU.bypass, replica_groups=[list(range(NCORES))],
            ins=[d_ccin.ap().opt()], outs=[d_ccout.ap().opt()]),
            waits=[st_cc], inc=s_cc, nochain=True)

        if "qkv0" in dbg:
            p.dma("gpsimd", dbg["qkv0"].ap(), d_qkv.ap()[0], s_st,
                  waits=[(s_st, p.count(s_st))])
        if "kvz" in dbg:
            p.dma("sync", dbg["kvz"].ap(), cc_sb, s_st,
                  waits=[ev_rel])

        p3_bar = [(s_st, p.count(s_st)), (s_ldg, p.count(s_ldg)),
                  (s_pe, p.count(s_pe)), (s_ac, p.count(s_ac)),
                  (s_vf, p.count(s_vf))]
        w2_bf = carve(AR, 0, (11, 2048), BF16)
        w13 = [carve(AR, 46080, (2, 16, 128), BF16),
               carve(AR, 54272, (2, 16, 128), BF16)]
        B_sb = carve(AR, 62464, (4, 512), BF16)
        tmpS = carve(AR, 66560, (512,), F32)
        yev = carve(AR, 68608, (2048,), BF16)
        xnh = carve(AR, 73728, (16, 2048), BF16)
        h_sb = carve(AR, 139264, (11, 2048), BF16)

        y_done = [None] * NCH
        done_pbfB = None
        done_Bsb = []
        done_hsb = None
        done_yev = None
        done_w2 = None
        done_w13 = [None, None]
        done_xnh = None
        done_p3pb = [None] * 8
        for ei in range(P3_PASSES):
            routed = ei > 0
            e_idx = ei - 1
            src_w1 = d_w1.ap()[e_idx] if routed else d_sw1.ap()
            src_w3 = d_w3.ap()[e_idx] if routed else d_sw3.ap()
            src_w2 = d_w2.ap()[e_idx] if routed else d_sw2.ap()
            ld_w2 = p.dma("gpsimd", w2_bf,
                          src_w2.rearrange("(k p) c -> p k c", p=128), s_ldg,
                          waits=(p3_bar if ei == 0 else [done_w2]))
            for half in range(2):
                ldxn = p.dma("sync", xnh,
                             d_xnfm.ap()[:, :, half * 2048:(half + 1) * 2048]
                             .rearrange("dt p c -> p dt c"), s_ld,
                             waits=([done_xnh] if done_xnh else p3_bar))
                ev_scale = None
                po_scale = None
                if routed:
                    ac_B = None
                    pbfF = pbf.ap().bitcast(F32)
                    for g4 in range(4):
                        g = half * 4 + g4
                        pe_B = p.mm(pbfF[:, 0:512],
                                    E8_sb[:, e_idx * 128:(e_idx + 1) * 128],
                                    alphasT[:, g * 512:(g + 1) * 512],
                                    True, True,
                                    waits=([done_pbfB] if (done_pbfB and g4 == 0)
                                           else ([ac_B] if ac_B else None)),
                                    inc=s_pe)
                        ac_B = p.a(lambda e, g4=g4, pbfF=pbfF: e.copy(
                            out=B_sb[:, g4, :], in_=pbfF[:, 0:512]),
                            waits=[pe_B] + (done_Bsb if g4 == 0 else []))
                    done_pbfB = ac_B
                    for k in range(ND):
                        for g4 in range(4):
                            fn = (lambda e, k=k, g4=g4: e.tensor_mul(
                                out=xnh[:, k, g4 * 512:(g4 + 1) * 512],
                                in0=xnh[:, k, g4 * 512:(g4 + 1) * 512],
                                in1=B_sb[:, g4, :]))
                            wt = [ldxn, ac_B] if g4 == 0 and k <= 1 else None
                            if k % 2 == 0:
                                ev_scale = p.v(fn, waits=wt)
                            else:
                                po_scale = p.g(fn, waits=wt)
                    done_Bsb = [ev_scale, po_scale]
                scale_bar = [m for m in (ldxn, ev_scale, po_scale) if m]
                if "xin0" in dbg and ei == 1 and half == 0:
                    p.dma("gpsimd", dbg["xin0"].ap(),
                          xnh.rearrange("p a b -> p (a b)"), s_st, waits=scale_bar)
                pe_h_last = None
                ev_h = None
                for f in range(NF):
                    wp = f % 2
                    p.dma("gpsimd", w13[wp][:, 0],
                          src_w1[:, f * 128:(f + 1) * 128]
                          .rearrange("(k p) c -> p k c", p=128), s_ldg,
                          waits=([done_w13[wp]] if done_w13[wp] else None))
                    ld13 = p.dma("gpsimd", w13[wp][:, 1],
                                 src_w3[:, f * 128:(f + 1) * 128]
                                 .rearrange("(k p) c -> p k c", p=128), s_ldg)
                    pA = pb[0 + 2 * (f % 2)]
                    pB = pb[1 + 2 * (f % 2)]
                    for g4 in range(4):
                        for k in range(ND):
                            w0_ = ([ld13] + scale_bar +
                                   [done_p3pb[0 + 2 * (f % 2)]]) if k == 0 else None
                            p.mm(pA[:, 0:512], w13[wp][:, 0, k],
                                 xnh[:, k, g4 * 512:(g4 + 1) * 512],
                                 k == 0, k == ND - 1, waits=w0_,
                                 inc=(s_pe if k == ND - 1 else None))
                        pe1 = (s_pe, p.count(s_pe))
                        for k in range(ND):
                            w0_ = ([done_p3pb[1 + 2 * (f % 2)]]) if k == 0 else None
                            pe_h_last = p.mm(pB[:, 0:512], w13[wp][:, 1, k],
                                             xnh[:, k, g4 * 512:(g4 + 1) * 512],
                                             k == 0, k == ND - 1, waits=w0_,
                                             inc=(s_pe if k == ND - 1 else None))
                        ac_si = p.a(lambda e, pA=pA: e.activation(
                            out=tmpS, in_=pA[:, 0:512], func=AF.Silu),
                            waits=[pe1] + ([ev_h] if ev_h else []))
                        ev_h = p.v(lambda e, f=f, g4=g4, pB=pB: e.tensor_tensor(
                            out=h_sb[:, f, g4 * 512:(g4 + 1) * 512], in0=tmpS,
                            in1=pB[:, 0:512], op=ALU.mult),
                            waits=[ac_si, pe_h_last] +
                                  ([done_hsb] if (done_hsb and f == 0 and g4 == 0)
                                   else []))
                        done_p3pb[0 + 2 * (f % 2)] = ac_si
                        done_p3pb[1 + 2 * (f % 2)] = ev_h
                    done_w13[wp] = pe_h_last
                done_xnh = pe_h_last
                for chh in range(16):
                    ch = half * 16 + chh
                    bs = 3 * (chh % 2)
                    pey = [None] * 4
                    for o in range(4):
                        for k in range(NF):
                            w0_ = None
                            if k == 0:
                                w0_ = [ev_h, ld_w2] +                                       ([done_p3pb[bs + o]] if done_p3pb[bs + o] else [])
                            pey[o] = p.mm(pb[bs + o][:, 0:512],
                                          h_sb[:, k, chh * 128:(chh + 1) * 128],
                                          w2_bf[:, k, o * 512:(o + 1) * 512],
                                          k == 0, k == NF - 1, waits=w0_,
                                          inc=(s_pe if k == NF - 1 else None))
                    evo = None
                    for o in range(4):
                        wy = [pey[o]] + ([done_yev] if done_yev else [])
                        if o < 2:
                            evo = p.v(lambda e, bs=bs, o=o: e.tensor_copy(
                                out=yev[:, o * 512:(o + 1) * 512],
                                in_=pb[bs + o][:, 0:512]), waits=wy)
                        else:
                            evo = p.a(lambda e, bs=bs, o=o: e.copy(
                                out=yev[:, o * 512:(o + 1) * 512],
                                in_=pb[bs + o][:, 0:512]), waits=wy)
                        done_p3pb[bs + o] = evo
                    yw = [evo, done_p3pb[bs], done_p3pb[bs + 1]]
                    if y_done[ch]:
                        yw.append(y_done[ch])
                    if routed:
                        y_done[ch] = p.dma("gpsimd",
                                           d_y.ap()[ch * 128:(ch + 1) * 128, :], yev,
                                           s_stg, waits=yw, accum_op=ALU.add)
                    else:
                        y_done[ch] = p.dma("sync",
                                           d_y.ap()[ch * 128:(ch + 1) * 128, :], yev,
                                           s_st, waits=yw)
                    done_yev = y_done[ch]
                done_hsb = pey[3]
            done_w2 = pey[3]

        if "hsb" in dbg:
            p.dma("gpsimd", dbg["hsb"].ap(), h_sb.rearrange("p a b -> p (a b)"), s_st,
                  waits=[(s_pe, p.count(s_pe)), (s_vf, p.count(s_vf))])
        if "yffn" in dbg:
            p.dma("gpsimd", dbg["yffn"].ap(), d_y.ap(), s_st,
                  waits=[m for m in y_done if m])

        cp_bar = [cc_done, (s_st, p.count(s_st)), (s_pe, p.count(s_pe)),
                  (s_vf, p.count(s_vf)), (s_ac, p.count(s_ac)),
                  (s_pf, p.count(s_pf)), (s_ldg, p.count(s_ldg))]
        kvz_base = carve(AR, 0, (2064,), F32)
        kvz_tmp = carve(AR, 8320, (2080,), F32)
        E16_sb = carve(AR, 16640, (2048,), F32)

        ld_e16 = p.dma("sync", E16_sb[0:16, :], d_E16.ap(), s_ld, waits=cp_bar)
        ev0m = p.v(lambda e: e.memset(kvz_base, 0.0), waits=cp_bar)
        ev_ra = p.v(lambda e: e.memset(small[0:4, 16:32], 0.0))
        pe_pm = p.mm(pb[0][:, 0:8], ones_f[:], prefm[:], True, False, waits=cp_bar)
        pe_pm = p.mm(pb[0][:, 8:16], ones_f[:], relm[:], False, True, inc=s_pe)
        ev_pm = p.v(lambda e: e.tensor_copy(out=small[0:128, 0:16]
                    if False else cs_bc[:, 0:8], in_=pb[0][:, 0:8]), waits=[pe_pm])
        ev_rm = p.v(lambda e: e.tensor_copy(out=bias_bc[:, 0:8], in_=pb[0][:, 8:16]))
        ld_t = None
        for r in range(NCORES):
            ld_t = p.dma("sync", kvz_tmp,
                         d_ccout.ap()[r * 128:(r + 1) * 128, :], s_ld,
                         waits=[(s_vf, p.count(s_vf))])
            p.v(lambda e, r=r: e.scalar_tensor_tensor(
                out=kvz_base, in0=kvz_tmp[:, 0:2064], scalar=cs_bc[:, r:r + 1],
                in1=kvz_base, op0=ALU.mult, op1=ALU.add), waits=[ld_t, ev_pm])
            ev_ra = p.v(lambda e, r=r: e.scalar_tensor_tensor(
                out=small[0:4, 16:32], in0=kvz_tmp[0:4, 2064:2080],
                scalar=bias_bc[0:4, r:r + 1], in1=small[0:4, 16:32],
                op0=ALU.mult, op1=ALU.add), waits=[ev_rm])
        pe_ns = p.mm(pb[1][0:1, 0:64], ones_bf[:], norms_bf[:], True, True,
                     waits=cp_bar, inc=s_pe)
        ev_ = p.v(lambda e: e.tensor_copy(out=small[0:1, 0:64], in_=pb[1][0:1, 0:64]),
                  waits=[pe_ns])
        p.v(lambda e: e.tensor_add(out=small[0:1, 0:32], in0=small[0:1, 0:32],
                                   in1=small[0:1, 32:64]))
        p.v(lambda e: e.tensor_add(out=small[0:1, 0:16], in0=small[0:1, 0:16],
                                   in1=small[0:1, 16:32]))
        p.v(lambda e: e.tensor_add(out=small[0:1, 0:8], in0=small[0:1, 0:8],
                                   in1=small[0:1, 8:16]))
        ev_sm = p.v(lambda e: e.tensor_add(out=small[0:1, 44:48], in0=small[0:1, 0:4],
                                           in1=small[0:1, 4:8]))
        pe_tc = p.mm(pb[1][0:4, 64:65], small[0:1, 44:48], one_f[:], True, True,
                     waits=[ev_sm], inc=s_pe)
        ev_tm = p.v(lambda e: e.tensor_scalar(
            out=small[0:4, 40:41], in0=pb[1][0:4, 64:65], scalar1=EPS_A,
            scalar2=NEG_BIG, op0=ALU.is_lt, op1=ALU.mult), waits=[pe_tc])
        ev_rel2 = p.v(lambda e: e.tensor_scalar(
            out=small[0:4, 16:32], in0=small[0:4, 16:32],
            scalar1=SCALE / 16384.0, scalar2=small[0:4, 40:41],
            op0=ALU.mult, op1=ALU.add), waits=[ev_ra, ev_tm])
        pe_rt = p.tp(pb[1][0:16, 0:4], small[0:4, 16:32], identf[0:4, 0:4],
                     waits=[ev_rel2], inc=s_pe)
        ev_w0 = p.v(lambda e: e.tensor_copy(out=relw[:, 0:4], in_=pb[1][0:16, 0:4]),
                    waits=[pe_rt])
        p.v(lambda e: e.reduce_max(out=relw[:, 4:5], in_=relw[:, 0:4], axis=AX.X))
        ev_nm = p.v(lambda e: e.tensor_scalar(
            out=relw[:, 5:6], in0=relw[:, 4:5], scalar1=-1.0, scalar2=None,
            op0=ALU.mult))
        ac_ex = p.a(lambda e: e.activation(out=relw[:, 8:12], in_=relw[:, 0:4],
                                           func=AF.Exp, bias=relw[:, 5:6]),
                    waits=[ev_nm])
        p.v(lambda e: e.reduce_sum(out=relw[:, 6:7], in_=relw[:, 8:12], axis=AX.X),
            waits=[ac_ex])
        p.v(lambda e: e.reciprocal(out=relw[:, 7:8], in_=relw[:, 6:7]))
        ev_wt = p.v(lambda e: e.tensor_scalar(
            out=relw[:, 8:12], in0=relw[:, 8:12], scalar1=relw[:, 7:8],
            scalar2=None, op0=ALU.mult))
        ac_g = p.a(lambda e: e.activation(out=small[0:1, 64:80], in_=gate_row[:],
                                          func=AF.Sigmoid), waits=cp_bar)
        ev_og = p.v(lambda e: e.tensor_scalar(
            out=small[0:1, 80:96], in0=small[0:1, 64:80], scalar1=-1.0, scalar2=1.0,
            op0=ALU.mult, op1=ALU.add), waits=[ac_g])
        pe_gc = p.mm(pb[1][0:16, 8:9], small[0:1, 64:80], one_f[:], True, True,
                     waits=[ac_g], inc=s_pe)
        ev_gc = p.v(lambda e: e.tensor_copy(out=relw[:, 12:13], in_=pb[1][0:16, 8:9]),
                    waits=[pe_gc])
        ev_At = p.v(lambda e: e.tensor_scalar(
            out=relw[:, 8:12], in0=relw[:, 8:12], scalar1=relw[:, 12:13],
            scalar2=None, op0=ALU.mult), waits=[ev_wt, ev_gc])
        for h in range(H):
            pe_A = p.mm(pb[2][:, 16 + h * 4:16 + (h + 1) * 4],
                        E16_sb[0:16, h * 128:(h + 1) * 128], relw[:, 8:12],
                        h == 0, False, waits=([ev_At, ld_e16] if h == 0 else None))
        pe_A = p.mm(pb[2][:, 0:16], ones_f[:], small[0:1, 80:96], False, True,
                    waits=[ev_og], inc=s_pe)
        ev_A = p.v(lambda e: e.tensor_copy(out=BG[:], in_=pb[2][:, 0:16]),
                   waits=[pe_A])
        for h in range(H):
            for r4 in range(4):
                ev_A = p.v(lambda e, h=h, r4=r4: e.tensor_copy(
                    out=A_rep[:, h * 16 + r4 * 4:h * 16 + (r4 + 1) * 4],
                    in_=pb[2][:, 16 + h * 4:16 + (h + 1) * 4]))
        bga = carve(AR, 114560, (128,), F32)
        p.v(lambda e: e.memset(bga, 1.0))
        for h in range(H):
            p.v(lambda e, h=h: e.tensor_copy(
                out=bga[:, h * 8:h * 8 + 1], in_=BG[:, h:h + 1]))
            bga_done = p.v(lambda e, h=h: e.tensor_copy(
                out=bga[:, h * 8 + 4:h * 8 + 8], in_=A_rep[:, h * 16:h * 16 + 4]))
        cp_done = [ev_A, bga_done, (s_vf, p.count(s_vf)), (s_ac, p.count(s_ac))]
        if "sbase" in dbg:
            p.dma("sync", dbg["sbase"].ap(), kvz_base, s_st,
                  waits=[(s_vf, p.count(s_vf))])
        if "wts" in dbg:
            p.dma("sync", dbg["wts"].ap(), relw[:, 8:12], s_st,
                  waits=[(s_vf, p.count(s_vf))])
        if "arep" in dbg:
            p.dma("sync", dbg["arep"].ap(), A_rep[:], s_st,
                  waits=[(s_vf, p.count(s_vf))])
        if "bg" in dbg:
            p.dma("sync", dbg["bg"].ap(), BG[:], s_st,
                  waits=[(s_vf, p.count(s_vf))])

        qkv_sb = [carve(AR, 24832, (8, 2052), BF16),
                  carve(AR, 57664, (8, 2052), BF16)]
        mem_rhs = carve(AR, 90496, (8192,), BF16)
        acc = carve(AR, 106880, (512,), F32)
        am23 = carve(AR, 108928, (1024,), BF16)
        attn_bf = carve(AR, 110976, (512,), BF16)
        attn_st = carve(AR, 112000, (512,), BF16)
        p2s = carve(AR, 113024, (64,), F32)
        sT_bf_t = carve(AR, 113280, (128,), BF16)
        attn_bf2 = carve(AR, 113536, (512,), BF16)

        ld_mem = p.dma("gpsimd", mem_rhs.rearrange("p (m h e) -> p m h e", m=M, h=H),
                       d_mem.ap().rearrange("m h d e -> d m h e"), s_ldg,
                       waits=cp_bar)

        Spsum = pb[0]
        numb = pb[1]
        denb = pb[2]
        sTb = pb[3]
        sTb_bf = pb[3].ap().bitcast(BF16)
        amb = [pb[4], pb[5], pb[6], pbf.ap().bitcast(F32)]
        attn_2 = [attn_bf, attn_bf2]
        done_qkv = [None, None]
        st_attn = None
        ev_mask = None
        t2e_prev = None
        ac_cast = None
        attn_done = [None, None]
        chmax_m = [None] * 4
        scale_m = [None] * 4
        madd_m = [[None] * 4 for _ in range(4)]
        pend_out = [None]

        def emit_out():
            nonlocal st_attn, t2e_prev
            hh, g4o, buf, miles = pend_out[0]
            pend_out[0] = None
            pe_t = None
            for ci in range(4):
                w = [miles[ci]]
                if ci == 0 and t2e_prev:
                    w.append(t2e_prev)
                pe_t = p.tp(sTb_bf[:, 256 + ci * 128:256 + (ci + 1) * 128],
                            attn_2[buf][:, ci * 128:(ci + 1) * 128], identb[:],
                            waits=w, inc=(s_pe if ci == 3 else None))
            t2e_prev = p.a(lambda e: e.copy(out=attn_st[:, 0:512],
                                            in_=sTb_bf[:, 256:768]),
                           waits=[pe_t] + ([st_attn] if st_attn else []))
            st_attn = p.dma("sync", d_attn.ap()[hh][:, g4o * 512:(g4o + 1) * 512],
                            attn_st[:, 0:512], s_st, waits=[t2e_prev])
            attn_done[buf] = pe_t

        for h in range(H):
            hp = h % 2
            ldq = p.dma("sync", qkv_sb[hp],
                        d_qkv.ap()[h].rearrange("g p c -> p g c"), s_ld,
                        waits=([done_qkv[hp]] if done_qkv[hp] else
                               [(s_st, p.count(s_st))]))
            pe_init = p.mm(Spsum[:, 0:129], identf[:],
                           kvz_base[:, h * 129:(h + 1) * 129], True, False,
                           waits=([ev0m, (s_vf, p.count(s_vf))] if h == 0 else
                                  [ac_cast]), inc=s_pe)
            pe_kv = pe_init
            ac_cast = p.a(lambda e: e.copy(out=S_bf[0][:], in_=Spsum[:, 0:129]),
                          waits=[pe_init])
            for g4 in range(8):
                buf = (h * 8 + g4) % 2
                for ci in range(4):
                    n = g4 * 4 + ci
                    qp_v = qkv_sb[hp][:, g4, ci * 128:(ci + 1) * 128]
                    kpf_v = qkv_sb[hp][:, g4, 512 + ci * 128:512 + (ci + 1) * 128]
                    kpt_v = qkv_sb[hp][:, g4, 1024 + ci * 128:1024 + (ci + 1) * 128]
                    va_v = qkv_sb[hp][:, g4, 1536 + ci * 129:1536 + (ci + 1) * 129]
                    v_v = qkv_sb[hp][:, g4, 1536 + ci * 129:1536 + ci * 129 + 128]
                    wsT = ([ev_mask] if ev_mask else [])
                    if g4 == 0 and ci == 0:
                        wsT.append(ldq)
                    pe_sT = p.mm(sTb[:, 0:128], kpf_v, qp_v, True, True,
                                 waits=wsT or None, inc=s_pe)
                    mfn = lambda e: e.tensor_tensor(
                        out=sT_bf_t, in0=sTb[:, 0:128], in1=trimask[:],
                        op=ALU.mult)
                    ev_mask = p.v(mfn, waits=[pe_sT])
                    if ci == 2 and pend_out[0] is not None:
                        emit_out()
                    for m in range(M):
                        wam = []
                        if ci == 0 and h == 0 and g4 == 0 and m == 0:
                            wam.append(ld_mem)
                        if madd_m[ci][m]:
                            wam.append(madd_m[ci][m])
                        p.mm(amb[m][:, ci * 128:(ci + 1) * 128], qp_v,
                             mem_rhs[:, ((m * H) + h) * 128:((m * H) + h + 1) * 128],
                             ci == 0, False, waits=wam or None)
                    wint = [ac_cast] + ([scale_m[ci]] if scale_m[ci] else [])
                    p.mm(numb[:, ci * 128:(ci + 1) * 128], qp_v,
                         S_bf[n % 2][:, 0:128], ci == 0, False, waits=wint)
                    p.mm(denb[:, ci * 8:ci * 8 + 1], qp_v, S_bf[n % 2][:, 128:129],
                         ci == 0, False,
                         waits=([chmax_m[ci]] if chmax_m[ci] else None))
                    p.mm(denb[:, ci * 8 + 4:ci * 8 + 8], qp_v,
                         norms_bf[:, h * 4:(h + 1) * 4], False, False,
                         waits=([ldg_norm] if h == 0 and g4 == 0 and ci == 0
                                else None))
                    p.mm(numb[:, ci * 128:(ci + 1) * 128], sT_bf_t, v_v,
                         False, False, waits=[ev_mask])
                    p.mm(denb[:, ci * 8:ci * 8 + 1], sT_bf_t, ones_bf[:],
                         False, False)
                    pe_kv = p.mm(Spsum[:, 0:129], kpt_v, va_v, False,
                                 h == H - 1 and n == NCH - 1, inc=s_pe)
                    if n < NCH - 1:
                        ac_cast = p.a(lambda e, nb=(n + 1) % 2: e.copy(
                            out=S_bf[nb][:], in_=Spsum[:, 0:129]), waits=[pe_kv])
                    chmax_m[ci] = p.v(lambda e, ci=ci: e.tensor_scalar(
                        out=p2s[:, ci * 8:(ci + 1) * 8],
                        in0=denb[:, ci * 8:(ci + 1) * 8], scalar1=EPS_A,
                        scalar2=None, op0=ALU.max), waits=[pe_kv])
                    wdiv = ([scale_m[ci]] if scale_m[ci] else [])
                    if h == 0 and g4 == 0:
                        wdiv += cp_done
                    p.v(lambda e, ci=ci: e.reciprocal(
                        out=p2s[:, 32 + ci * 8:40 + ci * 8],
                        in_=p2s[:, ci * 8:(ci + 1) * 8]), waits=wdiv or None)
                    div_m = p.v(lambda e, ci=ci, h=h: e.tensor_tensor(
                        out=p2s[:, 32 + ci * 8:40 + ci * 8],
                        in0=p2s[:, 32 + ci * 8:40 + ci * 8],
                        in1=bga[:, h * 8:(h + 1) * 8], op=ALU.mult))
                    wlo = [div_m, pe_kv] + ([madd_m[ci][3]] if madd_m[ci][3]
                                            else [])
                    ac_lo = p.a(lambda e, ci=ci: e.activation(
                        out=acc[:, ci * 128:(ci + 1) * 128],
                        in_=numb[:, ci * 128:(ci + 1) * 128], func=AF.Copy,
                        scale=p2s[:, 32 + ci * 8:33 + ci * 8]), waits=wlo)
                    scale_m[ci] = ac_lo
                    for m in range(M):
                        w_ = [ac_lo] if m == 0 else []
                        if m == 3 and attn_done[buf]:
                            w_.append(attn_done[buf])
                        outv = (acc[:, ci * 128:(ci + 1) * 128] if m < 3 else
                                attn_2[buf][:, ci * 128:(ci + 1) * 128])
                        madd_m[ci][m] = p.v(
                            lambda e, ci=ci, m=m, outv=outv:
                            e.scalar_tensor_tensor(
                                out=outv,
                                in0=amb[m][:, ci * 128:(ci + 1) * 128],
                                scalar=p2s[:, 36 + ci * 8 + m:37 + ci * 8 + m],
                                in1=acc[:, ci * 128:(ci + 1) * 128],
                                op0=ALU.mult, op1=ALU.add),
                            waits=w_ or None)
                pend_out[0] = (h, g4, buf, [madd_m[ci][3] for ci in range(4)])
            done_qkv[hp] = pe_kv
        emit_out()

        if "attn" in dbg:
            p.dma("gpsimd", dbg["attn"].ap(), d_attn.ap(), s_st,
                  waits=[(s_st, p.count(s_st))])

        p4_bar = [(s_st, p.count(s_st)), (s_pe, p.count(s_pe)),
                  (s_vf, p.count(s_vf)), (s_ac, p.count(s_ac)),
                  (s_pf, p.count(s_pf))]
        wo_bf = carve(AR, 0, (16, 2048), BF16)
        att4 = [carve(AR, 65536, (16, 128), BF16), carve(AR, 69632, (16, 128), BF16)]
        xc4 = [carve(AR, 73728, (2048,), F32), carve(AR, 81920, (2048,), F32)]
        yc4 = [carve(AR, 90112, (2048,), BF16), carve(AR, 94208, (2048,), BF16)]
        oc4 = [carve(AR, 98304, (2048,), F32), carve(AR, 106496, (2048,), F32)]

        ld_wo = p.dma("gpsimd", wo_bf,
                      d_wo.ap().rearrange("(k p) c -> p k c", p=128), s_ldg,
                      waits=p4_bar)
        done_a4 = [None, None]
        done_o4 = [None, None]
        last_out = None
        pey4 = [None] * 8
        done_p4pb = [None] * 8
        for c in range(NCH):
            bp = c % 2
            lda = p.dma("sync", att4[bp],
                        d_attn.ap()[:, :, c * 128:(c + 1) * 128]
                        .rearrange("h p c -> p h c"), s_ld,
                        waits=([done_a4[bp]] if done_a4[bp] else p4_bar))
            ldx = p.dma("sync", xc4[bp], d_x.ap()[c * 128:(c + 1) * 128, :], s_ld)
            ldy = p.dma("sync", yc4[bp], d_y.ap()[c * 128:(c + 1) * 128, :], s_ld,
                        waits=([y_done[c]]))
            bs = 3 * (c % 2)
            for o in range(4):
                for k in range(ND):
                    w0_ = None
                    if k == 0:
                        w0_ = [lda, ld_wo] +                               ([done_p4pb[bs + o]] if done_p4pb[bs + o] else [])
                    pey4[o] = p.mm(pb[bs + o][:, 0:512], att4[bp][:, k, :],
                                   wo_bf[:, k, o * 512:(o + 1) * 512],
                                   k == 0, k == ND - 1, waits=w0_,
                                   inc=(s_pe if k == ND - 1 else None))
            wo_done = pey4[3]
            for o in range(4):
                ev_s1 = p.v(lambda e, bs=bs, o=o, bp=bp: e.tensor_tensor(
                    out=oc4[bp][:, o * 512:(o + 1) * 512], in0=pb[bs + o][:, 0:512],
                    in1=xc4[bp][:, o * 512:(o + 1) * 512], op=ALU.add),
                    waits=[pey4[o], ldx] + ([done_o4[bp]] if (o == 0 and done_o4[bp])
                                            else []))
                done_p4pb[bs + o] = ev_s1
                po_s2 = p.g(lambda e, o=o, bp=bp: e.tensor_tensor(
                    out=oc4[bp][:, o * 512:(o + 1) * 512],
                    in0=oc4[bp][:, o * 512:(o + 1) * 512],
                    in1=yc4[bp][:, o * 512:(o + 1) * 512], op=ALU.add),
                    waits=[ev_s1, ldy])
            last_out = p.dma("sync", d_out.ap()[c * 128:(c + 1) * 128, :],
                             oc4[bp], s_out, waits=[po_s2])
            done_o4[bp] = last_out
            done_a4[bp] = wo_done
        p.op("gpsimd", lambda e: e.nop(), waits=[last_out], nochain=True)

        block = ctx.enter_context(nc.Block())
        p.emit(block)
    return nc


def _host_prep(inputs):
    x = np.ascontiguousarray(inputs["L0_attn_hidden_states"].reshape(B * S, D))
    shared = {
        "wq_t": np.ascontiguousarray(inputs["L0_attn_w_q"].T),
        "wk_t": np.ascontiguousarray(inputs["L0_attn_w_k"].T),
        "wv_t": np.ascontiguousarray(inputs["L0_attn_w_v"].T),
        "wo_t": np.ascontiguousarray(inputs["L0_attn_w_o"].T),
        "memories": np.ascontiguousarray(inputs["L0_attn_memories"]),
        "memnorms_t": np.ascontiguousarray(
            inputs["L0_attn_memory_norms"].transpose(2, 1, 0).reshape(HD, H * M)),
        "gate_row": np.ascontiguousarray(inputs["L0_attn_gate"].reshape(1, H)),
        "gate_wT": np.ascontiguousarray(inputs["L0_ffn_gate_w"].T),
        "ebias_row": np.ascontiguousarray(inputs["L0_ffn_expert_bias"].reshape(1, E)),
        "w1t": np.ascontiguousarray(inputs["L0_ffn_w1"].transpose(0, 2, 1)),
        "w3t": np.ascontiguousarray(inputs["L0_ffn_w3"].transpose(0, 2, 1)),
        "w2t": np.ascontiguousarray(inputs["L0_ffn_w2"].transpose(0, 2, 1)),
        "sw1t": np.ascontiguousarray(inputs["L0_ffn_sw1"].T),
        "sw3t": np.ascontiguousarray(inputs["L0_ffn_sw3"].T),
        "sw2t": np.ascontiguousarray(inputs["L0_ffn_sw2"].T),
        "identb": np.eye(128, dtype=BF),
        "identf": np.eye(128, dtype=np.float32),
        "trimask": np.triu(np.ones((128, 128), np.float32)).astype(BF),
        "E16": np.repeat(np.eye(16, dtype=np.float32)[:, :, None], 128, axis=2)
              .reshape(16, 16 * 128),
        "E8": np.repeat(np.eye(8, dtype=np.float32)[:, :, None], 128, axis=2)
              .reshape(8, 8 * 128).astype(BF),
    }
    in_maps = []
    for c in range(NCORES):
        b, q = divmod(c, 4)
        sl = x[c * T:(c + 1) * T]
        pref = np.zeros(8, np.float32)
        rl = np.zeros(8, np.float32)
        for qq in range(4):
            rl[b * 4 + qq] = 1.0
            if qq < q:
                pref[b * 4 + qq] = 1.0
        m = dict(shared)
        m["x_tm"] = sl
        m["x_fm"] = np.ascontiguousarray(sl.T)
        m["prefmask"] = pref.reshape(1, 8)
        m["relmask"] = rl.reshape(1, 8)
        in_maps.append(m)
    return in_maps


def kernel(**inputs):
    in_maps = _host_prep(inputs)
    nc = build()
    res = run_bass_kernel_spmd(nc, in_maps, core_ids=list(range(NCORES)))
    out = np.concatenate([r["out"] for r in res.results], axis=0)
    return out.reshape(B, S, D)


def run_debug(inputs, cores=(0,)):
    in_maps = _host_prep(inputs)
    nc = build()
    res = run_bass_kernel_spmd(nc, [in_maps[c] for c in cores], core_ids=list(cores))
    return res.results


# version 10 (final)
# speedup vs baseline: 1.2084x; 1.0215x over previous
import numpy as np
import ml_dtypes
from contextlib import ExitStack

import concourse.bass as bass
import concourse.mybir as mybir
from concourse.bass_utils import run_bass_kernel_spmd

F32 = mybir.dt.float32
BF16 = mybir.dt.bfloat16
I32 = mybir.dt.int32
AF = mybir.ActivationFunctionType
ALU = mybir.AluOpType
AX = mybir.AxisListType
BF = ml_dtypes.bfloat16

NCORES = 8
B, S, D = 2, 16384, 2048
T = 4096
NCH = 32
ND = 16
H, HD = 16, 128
M = 4
E = 8
F, NF = 1408, 11
SCALE = HD ** -0.5
LN_EPS = 1e-5
EPS_A = 1e-6
NEG_BIG = -1e30

DEBUG = ()
P3_PASSES = 9


class Prog:
    ENGINES = ("sync", "scalar", "vector", "tensor", "gpsimd")

    CHAIN = ("vector", "scalar", "gpsimd")

    def __init__(self, nc, chain_sems=None):
        self.nc = nc
        self.ops = {k: [] for k in self.ENGINES}
        self.cnt = {}
        self.chain_sem = chain_sems or {}

    def count(self, sem):
        return self.cnt.get(sem.name, 0)

    def op(self, eng, fn, waits=None, inc=None, inc16=None, nochain=False):
        waits = list(waits or [])
        chained = eng in self.CHAIN and self.chain_sem[eng] is not None
        if chained:
            cs = self.chain_sem[eng]
            waits.append((cs, self.count(cs)))
        w = []
        for w_ in waits:
            if w_ is None:
                continue
            (sem, val) = w_
            assert val <= self.count(sem), \
                f"wait on future count {sem.name} {val} > {self.count(sem)}"
            if val > 0:
                w.append((sem, val))
        incs = []
        assert inc is None or inc16 is None
        assert not (chained and not nochain and (inc is not None or inc16 is not None))
        mile = None
        if inc is not None:
            self.cnt[inc.name] = self.count(inc) + 1
            incs.append((inc, 1))
            mile = (inc, self.count(inc))
        elif inc16 is not None:
            self.cnt[inc16.name] = self.count(inc16) + 16
            incs.append((inc16, 16))
            mile = (inc16, self.count(inc16))
        elif chained and not nochain:
            cs = self.chain_sem[eng]
            self.cnt[cs.name] = self.count(cs) + 1
            incs.append((cs, 1))
            mile = (cs, self.count(cs))
        self.ops[eng].append((fn, w, incs))
        return mile

    def dma(self, q, out, in_, sem, waits=None, **kw):
        return self.op(q, (lambda e, o=out, i=in_, kw=kw:
                           e.dma_start(out=o, in_=i, **kw)),
                       waits=waits, inc16=sem, nochain=True)

    def mm(self, out, lhsT, rhs, start, stop, waits=None, inc=None):
        return self.op("tensor",
                       (lambda e, o=out, l=lhsT, r=rhs, st=start, sp=stop:
                        e.matmul(o, l, r, start=st, stop=sp)),
                       waits=waits, inc=inc)

    def tp(self, out, in_, ident, waits=None, inc=None):
        return self.op("tensor",
                       (lambda e, o=out, i=in_, d=ident: e.transpose(o, i, d)),
                       waits=waits, inc=inc)

    def v(self, fn, waits=None, inc=None):
        return self.op("vector", fn, waits=waits, inc=inc)

    def a(self, fn, waits=None, inc=None):
        return self.op("scalar", fn, waits=waits, inc=inc)

    def g(self, fn, waits=None, inc=None, nochain=False):
        return self.op("gpsimd", fn, waits=waits, inc=inc, nochain=nochain)

    def emit(self, block):
        for eng_name in self.ENGINES:
            ops = self.ops[eng_name]

            def body(eng, ops=ops):
                hwm = {}
                for (fn, waits, incs) in ops:
                    for (sem, val) in waits:
                        if val > hwm.get(sem.name, 0):
                            eng.wait_ge(sem, val)
                            hwm[sem.name] = val
                    inst = fn(eng)
                    for (sem, val) in incs:
                        inst = inst.then_inc(sem, val)

            getattr(block, eng_name)(body)


def build():
    nc = bass.Bass("TRN2", target_bir_lowering=False)

    d_x = nc.dram_tensor("x_tm", [T, D], F32, kind="ExternalInput")
    d_xfm = nc.dram_tensor("x_fm", [D, T], F32, kind="ExternalInput")
    d_wq = nc.dram_tensor("wq_t", [D, D], F32, kind="ExternalInput")
    d_wk = nc.dram_tensor("wk_t", [D, D], F32, kind="ExternalInput")
    d_wv = nc.dram_tensor("wv_t", [D, D], F32, kind="ExternalInput")
    d_wo = nc.dram_tensor("wo_t", [D, D], F32, kind="ExternalInput")
    d_mem = nc.dram_tensor("memories", [M, H, HD, HD], F32, kind="ExternalInput")
    d_mnorm = nc.dram_tensor("memnorms_t", [HD, H * M], F32, kind="ExternalInput")
    d_gate = nc.dram_tensor("gate_row", [1, H], F32, kind="ExternalInput")
    d_gw = nc.dram_tensor("gate_wT", [D, E], F32, kind="ExternalInput")
    d_ebias = nc.dram_tensor("ebias_row", [1, E], F32, kind="ExternalInput")
    d_w1 = nc.dram_tensor("w1t", [E, D, F], F32, kind="ExternalInput")
    d_w3 = nc.dram_tensor("w3t", [E, D, F], F32, kind="ExternalInput")
    d_w2 = nc.dram_tensor("w2t", [E, F, D], F32, kind="ExternalInput")
    d_sw1 = nc.dram_tensor("sw1t", [D, F], F32, kind="ExternalInput")
    d_sw3 = nc.dram_tensor("sw3t", [D, F], F32, kind="ExternalInput")
    d_sw2 = nc.dram_tensor("sw2t", [F, D], F32, kind="ExternalInput")
    d_identb = nc.dram_tensor("identb", [128, 128], BF16, kind="ExternalInput")
    d_identf = nc.dram_tensor("identf", [128, 128], F32, kind="ExternalInput")
    d_trimask = nc.dram_tensor("trimask", [128, 128], BF16, kind="ExternalInput")
    d_E16 = nc.dram_tensor("E16", [16, 16 * 128], F32, kind="ExternalInput")
    d_E8 = nc.dram_tensor("E8", [8, 8 * 128], BF16, kind="ExternalInput")
    d_prefmask = nc.dram_tensor("prefmask", [1, 8], F32, kind="ExternalInput")
    d_relmask = nc.dram_tensor("relmask", [1, 8], F32, kind="ExternalInput")

    d_xnfm = nc.dram_tensor("xnfm", [ND, 128, T], BF16)
    d_qkv = nc.dram_tensor("qkv", [H, 8, 128, 2052], BF16)
    d_attn = nc.dram_tensor("attnfm", [H, 128, T], BF16)
    d_y = nc.dram_tensor("ydram", [T, D], BF16)
    d_ccin = nc.dram_tensor("ccin", [128, 2080], F32)
    d_ccout = nc.dram_tensor("ccout", [8 * 128, 2080], F32)

    d_out = nc.dram_tensor("out", [T, D], F32, kind="ExternalOutput")
    dbg = {}
    for name, shape in (("xn", [T, D]), ("scores", [T, E]), ("alphas", [T, E]),
                        ("yffn", [T, D]), ("hsb", [128, 11 * 2048]), ("xin0", [128, 16 * 2048]),
                        ("attn", [H, 128, T]), ("den0", [128, 64]),
                        ("acc0", [128, 512]), ("sbf0", [128, 129]), ("spsum", [128, 129]),
                        ("vals8", [T, E]), ("ind", [T, E]),
                        ("xnsum", [128, ND]), ("alphasT", [8, T]),
                        ("qkv0", [8, 128, 2052]), ("kvz", [128, 2080]),
                        ("sbase", [128, 2064]), ("wts", [16, 4]),
                        ("arep", [128, 256]), ("bg", [128, 16])):
        if name in DEBUG:
            dbg[name] = nc.dram_tensor("dbg_" + name, shape, F32, kind="ExternalOutput")

    with ExitStack() as ctx:
        def sbuf(name, shape, dt=F32):
            return ctx.enter_context(nc.sbuf_tensor("sb_" + name, shape, dt))

        def psum(name, shape, dt=F32):
            return ctx.enter_context(nc.psum_tensor(name, shape, dt))

        def sem(name):
            return ctx.enter_context(nc.semaphore(name))

        identb = sbuf("identb", [128, 128], BF16)
        identf = sbuf("identf", [128, 128], F32)
        trimask = sbuf("trimask", [128, 128], BF16)
        ones_f = sbuf("ones_f", [1, 128], F32)
        ones_c = sbuf("ones_c", [128, 1], F32)
        ones_bf = sbuf("ones_bf", [128, 1], BF16)
        one_f = sbuf("one_f", [1, 1], F32)
        gate_row = sbuf("gate_row_sb", [1, H], F32)
        ebias_row = sbuf("ebias_sb", [1, E], F32)
        gw_sb = sbuf("gw_sb", [128, ND * E], F32)
        bias_bc = sbuf("bias_bc", [128, E], F32)
        cs_bc = sbuf("cs_bc", [128, E], F32)
        alphas = sbuf("alphas", [128, NCH * E], F32)
        alphasT = sbuf("alphasT", [8, T], BF16)
        xnsum = sbuf("xnsum", [128, ND], F32)
        xnsum_bf = sbuf("xnsum_bf", [128, ND], BF16)
        qsum_bf = sbuf("qsum_bf", [128, H], BF16)
        norms_bf = sbuf("norms_bf", [128, H * M], BF16)
        A_rep = sbuf("A_rep", [128, H * 16], F32)
        BG = sbuf("BG", [128, H], F32)
        S_bf = [sbuf("S_bf0", [128, 129], BF16), sbuf("S_bf1", [128, 129], BF16)]
        E8_sb = sbuf("E8_sb", [8, 8 * 128], BF16)
        prefm = sbuf("prefm", [1, 8], F32)
        relm = sbuf("relm", [1, 8], F32)
        small = sbuf("small", [16, 128], F32)
        relw = sbuf("relw", [16, 16], F32)

        AR = sbuf("AR", [128, 181 * 1024], mybir.dt.uint8)
        AR_M = ("seg", 0)
        AR_A = ("seg", 34816)
        AR_W = ("seg", 108544)

        def carve(arena, off_bytes, shape, dt):
            if isinstance(arena, tuple):
                off_bytes += arena[1]
            esz = {F32: 4, BF16: 2, I32: 4}[dt]
            assert off_bytes % esz == 0
            ap = AR.ap().bitcast(dt)
            n = int(np.prod(shape))
            v = ap[:, off_bytes // esz: off_bytes // esz + n]
            if len(shape) > 1:
                pat = " ".join(f"s{i}" for i in range(len(shape)))
                v = v.rearrange(f"p ({pat}) -> p {pat}",
                                **{f"s{i}": int(shape[i]) for i in range(len(shape))})
            return v

        pb = [psum(f"pb{i}", [128, 512], F32) for i in range(7)]
        pbf = psum("pbbf", [128, 1024], BF16)

        s_ld = sem("s_ld")
        s_ldg = sem("s_ldg")
        s_st = sem("s_st")
        s_stg = sem("s_stg")
        s_pe = sem("s_pe")
        s_ev = sem("s_ev")
        s_ac = sem("s_ac")
        s_po = sem("s_po")
        s_cc = sem("s_cc")
        s_out = sem("s_out")
        s_vf = sem("s_vf")
        s_pf = sem("s_pf")

        p = Prog(nc, chain_sems={"vector": s_vf, "scalar": s_ac,
                                 "gpsimd": s_pf})

        lds = []
        for dst, src in ((identb, d_identb), (identf, d_identf), (trimask, d_trimask),
                         (gate_row, d_gate), (ebias_row, d_ebias),
                         (prefm, d_prefmask), (relm, d_relmask)):
            lds.append(p.dma("sync", dst[:], src.ap(), s_ld))
        p.dma("sync", E8_sb[:], d_E8.ap(), s_ld)
        ld0 = p.dma("sync", gw_sb[:].rearrange("p (k e) -> p k e", e=E),
                    d_gw.ap().rearrange("(k p) e -> p k e", p=128), s_ld)
        ldg_norm = p.dma("gpsimd", norms_bf[:], d_mnorm.ap(), s_ldg)

        p.v(lambda e: e.memset(ones_f[:], 1.0))
        p.v(lambda e: e.memset(ones_c[:], 1.0))
        p.v(lambda e: e.memset(ones_bf[:], 1.0))
        p.v(lambda e: e.memset(one_f[:], 1.0))
        ev0 = p.v(lambda e: e.memset(xnsum[:], 0.0))

        p.mm(pb[0][:, 0:E], ones_f[:], ebias_row[:], True, True,
             waits=[ld0, ev0])
        for k in range(ND):
            pe_cs = p.mm(pb[1][0:1, 0:E], ones_c[:, 0:1], gw_sb[:, k * E:(k + 1) * E],
                         k == 0, k == ND - 1, inc=(s_pe if k == ND - 1 else None))
        p.v(lambda e: e.tensor_copy(out=bias_bc[:], in_=pb[0][:, 0:E]),
            waits=[pe_cs])
        ev_b = p.v(lambda e: e.tensor_copy(out=small[0:1, 0:E], in_=pb[1][0:1, 0:E]))
        pe_cb = p.mm(pb[0][:, 8:16], ones_f[:], small[0:1, 0:E], True, True,
                     waits=[ev_b], inc=s_pe)
        ev_cs = p.v(lambda e: e.tensor_copy(out=cs_bc[:], in_=pb[0][:, 8:16]),
                    waits=[pe_cb])

        xch = [carve(AR_M, 0, (2048,), F32), carve(AR_M, 8192, (2048,), F32)]
        xfmch = [carve(AR_M, 16384, (16, 128), F32), carve(AR_M, 24576, (16, 128), F32)]
        sqdump = carve(AR_A, 0, (2048,), F32)
        xnbf = carve(AR_A, 8192, (2048,), BF16)
        xnfmc = [carve(AR_A, 12288, (16, 128), BF16), carve(AR_A, 16384, (16, 128), BF16)]
        stat = carve(AR_A, 20480, (16,), F32)
        sc8 = carve(AR_A, 20544, (48,), F32)
        albf = carve(AR_A, 20736, (8,), BF16)
        xns_t = carve(AR_A, 20752, (16,), F32)
        dbgbuf = carve(AR_A, 24576, (2048,), F32)

        done_xc = [None, None]
        done_xf = [None, None]
        done_xnfm = [None, None]
        done_xnbf = None
        done_albf = None
        for c in range(NCH):
            bp = c % 2
            p.dma("sync", xch[bp], d_x.ap()[c * 128:(c + 1) * 128, :], s_ld,
                  waits=([done_xc[bp]] if done_xc[bp] else []))
            ldc = p.dma("sync", xfmch[bp],
                        d_xfm.ap()[:, c * 128:(c + 1) * 128]
                        .rearrange("(dt p) c -> p dt c", p=128), s_ld,
                        waits=([done_xf[bp]] if done_xf[bp] else []))

            xc = xch[bp]
            xf = xfmch[bp]
            mu = stat[:, 0:1]; rstd = stat[:, 1:2]; nrstd = stat[:, 2:3]
            ssq = stat[:, 3:4]; t0 = stat[:, 4:5]
            p.v(lambda e, xc=xc, t0=t0: e.reduce_sum(out=t0, in_=xc, axis=AX.X),
                waits=[ldc, ev_cs])
            p.v(lambda e, t0=t0, mu=mu: e.tensor_scalar(
                out=mu, in0=t0, scalar1=1.0 / D, scalar2=None, op0=ALU.mult))
            ev_xc = p.v(lambda e, xc=xc, mu=mu: e.tensor_scalar(
                out=xc, in0=xc, scalar1=mu, scalar2=None, op0=ALU.subtract))
            ac_sq = p.a(lambda e, xc=xc, ssq=ssq: e.activation(
                out=sqdump, in_=xc, func=AF.Square, accum_out=ssq),
                waits=[ev_xc])
            p.v(lambda e, ssq=ssq, t0=t0: e.tensor_scalar(
                out=t0, in0=ssq, scalar1=1.0 / D, scalar2=LN_EPS,
                op0=ALU.mult, op1=ALU.add), waits=[ac_sq])
            ev_t0 = p.v(lambda e: e.memset(small[0:1, 32:33], 0.0))
            ac_rt = p.a(lambda e, t0=t0: e.sqrt(out=t0, in_=t0), waits=[ev_t0])
            p.v(lambda e, t0=t0, rstd=rstd: e.reciprocal(out=rstd, in_=t0),
                waits=[ac_rt])
            p.v(lambda e, rstd=rstd, nrstd=nrstd: e.tensor_scalar(
                out=nrstd, in0=rstd, scalar1=-1.0, scalar2=None, op0=ALU.mult))
            ev_xn = p.v(lambda e, xc=xc, rstd=rstd: e.tensor_scalar(
                out=xnbf, in0=xc, scalar1=rstd, scalar2=None, op0=ALU.mult),
                waits=([done_xnbf] if done_xnbf else []))
            done_xc[bp] = ev_xn

            for k in range(ND):
                pe_u = p.mm(pb[2][:, 0:E], xf[:, k, :], gw_sb[:, k * E:(k + 1) * E],
                            k == 0, k == ND - 1,
                            waits=([ldc, ev_cs] if k == 0 else None),
                            inc=(s_pe if k == ND - 1 else None))
            done_xf[bp] = pe_u
            vsel = sc8[:, 0:8]; sco = sc8[:, 8:16]; vals8 = sc8[:, 16:24]
            ind = sc8[:, 24:32]
            ev_t = p.v(lambda e, vsel=vsel, mu=mu: e.scalar_tensor_tensor(
                out=vsel, in0=cs_bc[:], scalar=mu, in1=pb[2][:, 0:E],
                op0=ALU.mult, op1=ALU.subtract), waits=[pe_u])
            ac_sg = p.a(lambda e, sco=sco, vsel=vsel, nrstd=nrstd: e.activation(
                out=sco, in_=vsel, func=AF.Sigmoid, scale=nrstd),
                waits=[ev_t])
            p.v(lambda e, vsel=vsel, sco=sco: e.tensor_add(
                out=vsel, in0=sco, in1=bias_bc[:]), waits=[ac_sg])
            p.v(lambda e, vals8=vals8, vsel=vsel: e.max(out=vals8, in_=vsel))
            p.v(lambda e, ind=ind, vsel=vsel, vals8=vals8: e.tensor_scalar(
                out=ind, in0=vsel, scalar1=vals8[:, 1:2], scalar2=None, op0=ALU.is_ge))
            ev_al = p.v(lambda e, ind=ind, sco=sco, c=c: e.tensor_mul(
                out=alphas[:, c * E:(c + 1) * E], in0=ind, in1=sco))

            ac_ab = p.a(lambda e, c=c: e.copy(
                out=albf, in_=alphas[:, c * E:(c + 1) * E]),
                waits=[ev_al] + ([done_albf] if done_albf else []))
            pe_at = p.tp(pbf[0:8, 0:128], albf, identb[:],
                         waits=[(s_ac, p.count(s_ac))], inc=s_pe)
            done_albf = pe_at
            ac_at = p.a(lambda e, c=c: e.copy(
                out=alphasT[:, c * 128:(c + 1) * 128], in_=pbf[0:8, 0:128]),
                waits=[pe_at])

            wspill = [done_xnfm[bp]] if done_xnfm[bp] else []
            for half in range(2):
                for t8 in range(8):
                    dt_i = half * 8 + t8
                    wts_ = None
                    if t8 == 0:
                        wts_ = ([ev_xn, (s_ac, p.count(s_ac))]
                                + (wspill if half == 0 else []))
                    pe_t = p.tp(pbf[:, t8 * 128:(t8 + 1) * 128],
                                xnbf[:, dt_i * 128:(dt_i + 1) * 128],
                                identb[:], waits=wts_,
                                inc=(s_pe if t8 == 7 else None))
                ac_e = p.a(lambda e, half=half, bp=bp: e.copy(
                    out=xnfmc[bp][:, half * 8:(half + 1) * 8, :]
                    .rearrange("p a b -> p (a b)"),
                    in_=pbf[:, 0:1024]), waits=[pe_t])
            done_xnbf = pe_t
            ac_t = (s_ac, p.count(s_ac))
            p.v(lambda e, bp=bp: e.tensor_reduce(
                out=xns_t, in_=xnfmc[bp], axis=AX.X, op=ALU.add),
                waits=[ac_t])
            p.v(lambda e: e.tensor_add(out=xnsum[:], in0=xnsum[:], in1=xns_t))
            st_x = p.dma("sync", d_xnfm.ap()[:, :, c * 128:(c + 1) * 128]
                         .rearrange("dt p c -> p dt c"),
                         xnfmc[bp], s_st, waits=[ac_t])
            done_xnfm[bp] = st_x

            if "xn" in dbg:
                ev_d = p.v(lambda e: e.tensor_copy(out=dbgbuf, in_=xnbf), waits=[ev_xn])
                p.dma("sync", dbg["xn"].ap()[c * 128:(c + 1) * 128, :], dbgbuf, s_st,
                      waits=[ev_d])
            if "scores" in dbg:
                p.dma("sync", dbg["scores"].ap()[c * 128:(c + 1) * 128, :], sco, s_st,
                      waits=[ev_al])
            if "vals8" in dbg:
                p.dma("sync", dbg["vals8"].ap()[c * 128:(c + 1) * 128, :], vals8, s_st,
                      waits=[ev_al])
                p.dma("sync", dbg["ind"].ap()[c * 128:(c + 1) * 128, :], ind, s_st,
                      waits=[ev_al])
            if "alphas" in dbg:
                p.dma("sync", dbg["alphas"].ap()[c * 128:(c + 1) * 128, :],
                      alphas[:, c * E:(c + 1) * E], s_st, waits=[ev_al])

        ev_last = p.v(lambda e: e.memset(small[0:1, 33:34], 0.0))
        ac_xs = p.a(lambda e: e.copy(out=xnsum_bf[:], in_=xnsum[:]),
                    waits=[ev_last])
        if "xnsum" in dbg:
            p.dma("sync", dbg["xnsum"].ap(), xnsum[:], s_st, waits=[ev_last])
        if "alphasT" in dbg:
            ev_d = p.v(lambda e: e.tensor_copy(
                out=carve(AR_M, 0, (4096,), F32)[0:8, :], in_=alphasT[:]),
                waits=[(s_ac, p.count(s_ac))])
            p.dma("sync", dbg["alphasT"].ap(), carve(AR_M, 0, (4096,), F32)[0:8, :],
                  s_st, waits=[ev_d])

        p1_bar = [done_xnfm[0], done_xnfm[1], done_xnbf, ac_xs]

        wqkv = carve(AR_W, 0, (3, 4, 16, 128), BF16)
        xng = [carve(AR_A, 0, (16, 512), BF16),
               carve(AR_A, 16384, (16, 512), BF16)]
        stage = [carve(AR_A, 32768, (2052,), BF16),
                 carve(AR_A, 36872, (2052,), BF16)]
        tmpA = carve(AR_A, 40976, (512,), F32)
        tmpB = carve(AR_A, 43024, (512,), F32)
        cc_sb = carve(AR_A, 45072, (2080,), F32)

        for i in range(2):
            for ci in range(4):
                p.v(lambda e, i=i, ci=ci: e.memset(
                    stage[i][:, 1536 + ci * 129 + 128: 1536 + ci * 129 + 129], 1.0),
                    waits=(p1_bar if (i == 0 and ci == 0) else None))

        done_xng = [None, None]
        done_stage = [None, None]
        done_w = [None, None, None]
        done_pb = {0: None, 1: None, 2: None}
        done_kvz = None
        ac_qs = None
        for hs in range(4):
            heads = [hs * 4 + i for i in range(4)]
            wld = None
            for mi, dw in enumerate((d_wq, d_wk, d_wv)):
                for hh in range(4):
                    h = heads[hh]
                    wld = p.dma("gpsimd", wqkv[:, mi, hh],
                                dw.ap()[:, h * 128:(h + 1) * 128]
                                .rearrange("(k p) c -> p k c", p=128),
                                s_ldg,
                                waits=((p1_bar if hs == 0 else []) +
                                       ([done_w[mi]] if done_w[mi] else [])
                                       if hh == 0 else None))
            for hh in range(4):
                for k in range(ND):
                    pe_qs = p.mm(pb[2][:, hh:hh + 1], wqkv[:, 0, hh, k],
                                 xnsum_bf[:, k:k + 1],
                                 hh == 0 and k == 0, hh == 3 and k == ND - 1,
                                 waits=([wld, ac_xs] +
                                        ([done_pb[2]] if done_pb[2] else [])
                                        if (hh == 0 and k == 0) else None),
                                 inc=(s_pe if (hh == 3 and k == ND - 1) else None))
            ac_qs = p.a(lambda e, hs=hs: e.copy(
                out=qsum_bf[:, hs * 4:hs * 4 + 4], in_=pb[2][:, 0:4]),
                waits=[pe_qs])
            done_pb[2] = ac_qs

            pe_kvg = None
            for g in range(8):
                bp = g % 2
                ldx = p.dma("sync", xng[bp],
                            d_xnfm.ap()[:, :, g * 512:(g + 1) * 512]
                            .rearrange("dt p c -> p dt c"), s_ld,
                            waits=([done_xng[bp]] if done_xng[bp]
                                   else (p1_bar if hs == 0 else [])))
                pe_last_v = None
                for hh in range(4):
                    h = heads[hh]
                    sp = (g * 4 + hh) % 2
                    wst = ([x for x in (done_stage[sp],) if x] or [])
                    for k in range(ND):
                        pe_q = p.mm(pb[0][:, 0:512], wqkv[:, 0, hh, k],
                                    xng[bp][:, k, :], k == 0, k == ND - 1,
                                    waits=(([ldx, wld] +
                                            ([done_pb[0]] if done_pb[0] else []))
                                           if k == 0 else None),
                                    inc=(s_pe if k == ND - 1 else None))
                    ev_m = p.v(lambda e: e.tensor_scalar(
                        out=tmpA, in0=pb[0][:, 0:512], scalar1=0.0, scalar2=None,
                        op0=ALU.min), waits=[pe_q] + wst)
                    ac_e = p.a(lambda e: e.activation(out=tmpB, in_=tmpA, func=AF.Exp),
                               waits=[ev_m])
                    ev_qp = p.v(lambda e, sp=sp: e.scalar_tensor_tensor(
                        out=stage[sp][:, 0:512], in0=pb[0][:, 0:512], scalar=0.0,
                        in1=tmpB, op0=ALU.max, op1=ALU.add), waits=[ac_e])
                    done_pb[0] = ev_qp
                    for k in range(ND):
                        pe_k = p.mm(pb[1][:, 0:512], wqkv[:, 1, hh, k],
                                    xng[bp][:, k, :], k == 0, k == ND - 1,
                                    waits=(([done_pb[1]] if done_pb[1] else [])
                                           if k == 0 else None),
                                    inc=(s_pe if k == ND - 1 else None))
                    ev_m = p.v(lambda e: e.tensor_scalar(
                        out=tmpA, in0=pb[1][:, 0:512], scalar1=0.0, scalar2=None,
                        op0=ALU.min), waits=[pe_k])
                    ac_e = p.a(lambda e: e.activation(out=tmpB, in_=tmpA, func=AF.Exp),
                               waits=[ev_m])
                    ev_kp = p.v(lambda e, sp=sp: e.scalar_tensor_tensor(
                        out=stage[sp][:, 512:1024], in0=pb[1][:, 0:512], scalar=0.0,
                        in1=tmpB, op0=ALU.max, op1=ALU.add), waits=[ac_e])
                    done_pb[1] = ev_kp
                    for ci in range(4):
                        pe_tp = p.tp(pbf[:, ci * 128:(ci + 1) * 128],
                                     stage[sp][:, 512 + ci * 128:512 + (ci + 1) * 128],
                                     identb[:],
                                     waits=([ev_kp, (s_ac, p.count(s_ac))]
                                            if ci == 0 else None),
                                     inc=(s_pe if ci == 3 else None))
                    ac_tp = p.a(lambda e, sp=sp: e.copy(
                        out=stage[sp][:, 1024:1536], in_=pbf[:, 0:512]),
                        waits=[pe_tp])
                    ac_v = None
                    for cpair in range(2):
                        for ci2 in range(2):
                            ci = cpair * 2 + ci2
                            for k in range(ND):
                                pe_v = p.mm(pb[2][:, ci2 * 128:(ci2 + 1) * 128],
                                            xng[bp][:, k, ci * 128:(ci + 1) * 128],
                                            wqkv[:, 2, hh, k],
                                            k == 0 and ci2 == 0, k == ND - 1,
                                            waits=(([done_pb[2]]
                                                    if (done_pb[2] and ci2 == 0)
                                                    else None) if k == 0 else None),
                                            inc=(s_pe if (k == ND - 1 and ci2 == 1)
                                                 else None))
                        ac_v = p.a(lambda e, sp=sp, cpair=cpair: e.copy(
                            out=stage[sp][:, 1536 + cpair * 258:
                                          1536 + (cpair + 1) * 258]
                            .rearrange("p (a b) -> p a b", b=129)[:, :, 0:128],
                            in_=pb[2][:, 0:256]
                            .rearrange("p (a b) -> p a b", b=128)), waits=[pe_v])
                        done_pb[2] = ac_v
                    last_q, last_k, last_v = pe_q, pe_k, pe_v
                    for ci in range(4):
                        pe_kvg = p.mm(pb[3 + hh][:, 0:129],
                                      stage[sp][:, 1024 + ci * 128:1024 + (ci + 1) * 128],
                                      stage[sp][:, 1536 + ci * 129:1536 + (ci + 1) * 129],
                                      g == 0 and ci == 0, g == 7 and ci == 3,
                                      waits=(([ac_tp, ac_v] +
                                              ([done_kvz] if (g == 0 and ci == 0)
                                               else []))
                                             if ci == 0 else None),
                                      inc=(s_pe if ci == 3 else None))
                    st_sp = p.dma("sync", d_qkv.ap()[h, g], stage[sp], s_st,
                                  waits=[ev_qp, ac_tp, ac_v, pe_kvg])
                    done_stage[sp] = st_sp
                    pe_last_v = pe_kvg
                done_xng[bp] = pe_last_v
            done_w = [last_q, last_k, last_v]
            ev_kvz = None
            for hh in range(4):
                ev_kvz = p.v(lambda e, hs=hs, hh=hh: e.tensor_copy(
                    out=cc_sb[:, (hs * 4 + hh) * 129:(hs * 4 + hh + 1) * 129],
                    in_=pb[3 + hh][:, 0:129]), waits=[pe_last_v])
            done_kvz = ev_kvz

        for h in range(H):
            pe_rel = p.mm(pb[0][0:4, h:h + 1], norms_bf[:, h * 4:(h + 1) * 4],
                          qsum_bf[:, h:h + 1], h == 0, h == H - 1,
                          waits=([ldg_norm, ac_qs, done_pb[0]] if h == 0 else None),
                          inc=(s_pe if h == H - 1 else None))
        ev_rel = p.v(lambda e: e.tensor_copy(
            out=cc_sb[0:4, 2064:2080], in_=pb[0][0:4, 0:16]), waits=[pe_rel])
        st_cc = p.dma("sync", d_ccin.ap(), cc_sb, s_st, waits=[ev_rel])
        cc_done = p.g(lambda e: e.collective_compute(
            "AllGather", ALU.bypass, replica_groups=[list(range(NCORES))],
            ins=[d_ccin.ap().opt()], outs=[d_ccout.ap().opt()]),
            waits=[st_cc], inc=s_cc, nochain=True)

        if "qkv0" in dbg:
            p.dma("gpsimd", dbg["qkv0"].ap(), d_qkv.ap()[0], s_st,
                  waits=[(s_st, p.count(s_st))])
        if "kvz" in dbg:
            p.dma("sync", dbg["kvz"].ap(), cc_sb, s_st,
                  waits=[ev_rel])

        p3_bar = [(s_st, p.count(s_st)), (s_ldg, p.count(s_ldg)),
                  (s_pe, p.count(s_pe)), (s_ac, p.count(s_ac)),
                  (s_vf, p.count(s_vf))]
        w2_bf = carve(AR, 0, (11, 2048), BF16)
        w13 = [carve(AR, 46080, (2, 16, 128), BF16),
               carve(AR, 54272, (2, 16, 128), BF16)]
        B_sb = carve(AR, 62464, (4, 512), BF16)
        tmpS = carve(AR, 66560, (512,), F32)
        yev = carve(AR, 68608, (2048,), BF16)
        xnh = carve(AR, 73728, (16, 2048), BF16)
        h_sb = carve(AR, 139264, (11, 2048), BF16)

        y_done = [None] * NCH
        done_pbfB = None
        done_Bsb = []
        done_hsb = None
        done_yev = None
        done_w2 = None
        done_w13 = [None, None]
        done_xnh = None
        done_p3pb = [None] * 8
        for ei in range(P3_PASSES):
            routed = ei > 0
            e_idx = ei - 1
            src_w1 = d_w1.ap()[e_idx] if routed else d_sw1.ap()
            src_w3 = d_w3.ap()[e_idx] if routed else d_sw3.ap()
            src_w2 = d_w2.ap()[e_idx] if routed else d_sw2.ap()
            ld_w2 = p.dma("gpsimd", w2_bf,
                          src_w2.rearrange("(k p) c -> p k c", p=128), s_ldg,
                          waits=(p3_bar if ei == 0 else [done_w2]))
            for half in range(2):
                ldxn = p.dma("sync", xnh,
                             d_xnfm.ap()[:, :, half * 2048:(half + 1) * 2048]
                             .rearrange("dt p c -> p dt c"), s_ld,
                             waits=([done_xnh] if done_xnh else p3_bar))
                ev_scale = None
                po_scale = None
                if routed:
                    ac_B = None
                    pbfF = pbf.ap().bitcast(F32)
                    for g4 in range(4):
                        g = half * 4 + g4
                        pe_B = p.mm(pbfF[:, 0:512],
                                    E8_sb[:, e_idx * 128:(e_idx + 1) * 128],
                                    alphasT[:, g * 512:(g + 1) * 512],
                                    True, True,
                                    waits=([done_pbfB] if (done_pbfB and g4 == 0)
                                           else ([ac_B] if ac_B else None)),
                                    inc=s_pe)
                        ac_B = p.a(lambda e, g4=g4, pbfF=pbfF: e.copy(
                            out=B_sb[:, g4, :], in_=pbfF[:, 0:512]),
                            waits=[pe_B] + (done_Bsb if g4 == 0 else []))
                    done_pbfB = ac_B
                    for k in range(ND):
                        for g4 in range(4):
                            fn = (lambda e, k=k, g4=g4: e.tensor_mul(
                                out=xnh[:, k, g4 * 512:(g4 + 1) * 512],
                                in0=xnh[:, k, g4 * 512:(g4 + 1) * 512],
                                in1=B_sb[:, g4, :]))
                            wt = [ldxn, ac_B] if g4 == 0 and k <= 1 else None
                            if k % 2 == 0:
                                ev_scale = p.v(fn, waits=wt)
                            else:
                                po_scale = p.g(fn, waits=wt)
                    done_Bsb = [ev_scale, po_scale]
                scale_bar = [m for m in (ldxn, ev_scale, po_scale) if m]
                if "xin0" in dbg and ei == 1 and half == 0:
                    p.dma("gpsimd", dbg["xin0"].ap(),
                          xnh.rearrange("p a b -> p (a b)"), s_st, waits=scale_bar)
                pe_h_last = None
                ev_h = None
                for f in range(NF):
                    wp = f % 2
                    p.dma("gpsimd", w13[wp][:, 0],
                          src_w1[:, f * 128:(f + 1) * 128]
                          .rearrange("(k p) c -> p k c", p=128), s_ldg,
                          waits=([done_w13[wp]] if done_w13[wp] else None))
                    ld13 = p.dma("gpsimd", w13[wp][:, 1],
                                 src_w3[:, f * 128:(f + 1) * 128]
                                 .rearrange("(k p) c -> p k c", p=128), s_ldg)
                    pA = pb[0 + 2 * (f % 2)]
                    pB = pb[1 + 2 * (f % 2)]
                    for g4 in range(4):
                        for k in range(ND):
                            w0_ = ([ld13] + scale_bar +
                                   [done_p3pb[0 + 2 * (f % 2)]]) if k == 0 else None
                            p.mm(pA[:, 0:512], w13[wp][:, 0, k],
                                 xnh[:, k, g4 * 512:(g4 + 1) * 512],
                                 k == 0, k == ND - 1, waits=w0_,
                                 inc=(s_pe if k == ND - 1 else None))
                        pe1 = (s_pe, p.count(s_pe))
                        for k in range(ND):
                            w0_ = ([done_p3pb[1 + 2 * (f % 2)]]) if k == 0 else None
                            pe_h_last = p.mm(pB[:, 0:512], w13[wp][:, 1, k],
                                             xnh[:, k, g4 * 512:(g4 + 1) * 512],
                                             k == 0, k == ND - 1, waits=w0_,
                                             inc=(s_pe if k == ND - 1 else None))
                        ac_si = p.a(lambda e, pA=pA: e.activation(
                            out=tmpS, in_=pA[:, 0:512], func=AF.Silu),
                            waits=[pe1] + ([ev_h] if ev_h else []))
                        ev_h = p.v(lambda e, f=f, g4=g4, pB=pB: e.tensor_tensor(
                            out=h_sb[:, f, g4 * 512:(g4 + 1) * 512], in0=tmpS,
                            in1=pB[:, 0:512], op=ALU.mult),
                            waits=[ac_si, pe_h_last] +
                                  ([done_hsb] if (done_hsb and f == 0 and g4 == 0)
                                   else []))
                        done_p3pb[0 + 2 * (f % 2)] = ac_si
                        done_p3pb[1 + 2 * (f % 2)] = ev_h
                    done_w13[wp] = pe_h_last
                done_xnh = pe_h_last
                for chh in range(16):
                    ch = half * 16 + chh
                    bs = 3 * (chh % 2)
                    pey = [None] * 4
                    for o in range(4):
                        for k in range(NF):
                            w0_ = None
                            if k == 0:
                                w0_ = [ev_h, ld_w2] +                                       ([done_p3pb[bs + o]] if done_p3pb[bs + o] else [])
                            pey[o] = p.mm(pb[bs + o][:, 0:512],
                                          h_sb[:, k, chh * 128:(chh + 1) * 128],
                                          w2_bf[:, k, o * 512:(o + 1) * 512],
                                          k == 0, k == NF - 1, waits=w0_,
                                          inc=(s_pe if k == NF - 1 else None))
                    evo = None
                    for o in range(4):
                        wy = [pey[o]] + ([done_yev] if done_yev else [])
                        if o < 2:
                            evo = p.v(lambda e, bs=bs, o=o: e.tensor_copy(
                                out=yev[:, o * 512:(o + 1) * 512],
                                in_=pb[bs + o][:, 0:512]), waits=wy)
                        else:
                            evo = p.a(lambda e, bs=bs, o=o: e.copy(
                                out=yev[:, o * 512:(o + 1) * 512],
                                in_=pb[bs + o][:, 0:512]), waits=wy)
                        done_p3pb[bs + o] = evo
                    yw = [evo, done_p3pb[bs], done_p3pb[bs + 1]]
                    if y_done[ch]:
                        yw.append(y_done[ch])
                    if routed:
                        y_done[ch] = p.dma("gpsimd",
                                           d_y.ap()[ch * 128:(ch + 1) * 128, :], yev,
                                           s_stg, waits=yw, accum_op=ALU.add)
                    else:
                        y_done[ch] = p.dma("sync",
                                           d_y.ap()[ch * 128:(ch + 1) * 128, :], yev,
                                           s_st, waits=yw)
                    done_yev = y_done[ch]
                done_hsb = pey[3]
            done_w2 = pey[3]

        if "hsb" in dbg:
            p.dma("gpsimd", dbg["hsb"].ap(), h_sb.rearrange("p a b -> p (a b)"), s_st,
                  waits=[(s_pe, p.count(s_pe)), (s_vf, p.count(s_vf))])
        if "yffn" in dbg:
            p.dma("gpsimd", dbg["yffn"].ap(), d_y.ap(), s_st,
                  waits=[m for m in y_done if m])

        cp_bar = [cc_done, (s_st, p.count(s_st)), (s_pe, p.count(s_pe)),
                  (s_vf, p.count(s_vf)), (s_ac, p.count(s_ac)),
                  (s_pf, p.count(s_pf)), (s_ldg, p.count(s_ldg))]
        kvz_base = carve(AR, 0, (2064,), F32)
        kvz_tmp = carve(AR, 8320, (2080,), F32)
        E16_sb = carve(AR, 16640, (2048,), F32)

        ld_e16 = p.dma("sync", E16_sb[0:16, :], d_E16.ap(), s_ld, waits=cp_bar)
        ev0m = p.v(lambda e: e.memset(kvz_base, 0.0), waits=cp_bar)
        ev_ra = p.v(lambda e: e.memset(small[0:4, 16:32], 0.0))
        pe_pm = p.mm(pb[0][:, 0:8], ones_f[:], prefm[:], True, False, waits=cp_bar)
        pe_pm = p.mm(pb[0][:, 8:16], ones_f[:], relm[:], False, True, inc=s_pe)
        ev_pm = p.v(lambda e: e.tensor_copy(out=small[0:128, 0:16]
                    if False else cs_bc[:, 0:8], in_=pb[0][:, 0:8]), waits=[pe_pm])
        ev_rm = p.v(lambda e: e.tensor_copy(out=bias_bc[:, 0:8], in_=pb[0][:, 8:16]))
        ld_t = None
        for r in range(NCORES):
            ld_t = p.dma("sync", kvz_tmp,
                         d_ccout.ap()[r * 128:(r + 1) * 128, :], s_ld,
                         waits=[(s_vf, p.count(s_vf))])
            p.v(lambda e, r=r: e.scalar_tensor_tensor(
                out=kvz_base, in0=kvz_tmp[:, 0:2064], scalar=cs_bc[:, r:r + 1],
                in1=kvz_base, op0=ALU.mult, op1=ALU.add), waits=[ld_t, ev_pm])
            ev_ra = p.v(lambda e, r=r: e.scalar_tensor_tensor(
                out=small[0:4, 16:32], in0=kvz_tmp[0:4, 2064:2080],
                scalar=bias_bc[0:4, r:r + 1], in1=small[0:4, 16:32],
                op0=ALU.mult, op1=ALU.add), waits=[ev_rm])
        pe_ns = p.mm(pb[1][0:1, 0:64], ones_bf[:], norms_bf[:], True, True,
                     waits=cp_bar, inc=s_pe)
        ev_ = p.v(lambda e: e.tensor_copy(out=small[0:1, 0:64], in_=pb[1][0:1, 0:64]),
                  waits=[pe_ns])
        p.v(lambda e: e.tensor_add(out=small[0:1, 0:32], in0=small[0:1, 0:32],
                                   in1=small[0:1, 32:64]))
        p.v(lambda e: e.tensor_add(out=small[0:1, 0:16], in0=small[0:1, 0:16],
                                   in1=small[0:1, 16:32]))
        p.v(lambda e: e.tensor_add(out=small[0:1, 0:8], in0=small[0:1, 0:8],
                                   in1=small[0:1, 8:16]))
        ev_sm = p.v(lambda e: e.tensor_add(out=small[0:1, 44:48], in0=small[0:1, 0:4],
                                           in1=small[0:1, 4:8]))
        pe_tc = p.mm(pb[1][0:4, 64:65], small[0:1, 44:48], one_f[:], True, True,
                     waits=[ev_sm], inc=s_pe)
        ev_tm = p.v(lambda e: e.tensor_scalar(
            out=small[0:4, 40:41], in0=pb[1][0:4, 64:65], scalar1=EPS_A,
            scalar2=NEG_BIG, op0=ALU.is_lt, op1=ALU.mult), waits=[pe_tc])
        ev_rel2 = p.v(lambda e: e.tensor_scalar(
            out=small[0:4, 16:32], in0=small[0:4, 16:32],
            scalar1=SCALE / 16384.0, scalar2=small[0:4, 40:41],
            op0=ALU.mult, op1=ALU.add), waits=[ev_ra, ev_tm])
        pe_rt = p.tp(pb[1][0:16, 0:4], small[0:4, 16:32], identf[0:4, 0:4],
                     waits=[ev_rel2], inc=s_pe)
        ev_w0 = p.v(lambda e: e.tensor_copy(out=relw[:, 0:4], in_=pb[1][0:16, 0:4]),
                    waits=[pe_rt])
        p.v(lambda e: e.reduce_max(out=relw[:, 4:5], in_=relw[:, 0:4], axis=AX.X))
        ev_nm = p.v(lambda e: e.tensor_scalar(
            out=relw[:, 5:6], in0=relw[:, 4:5], scalar1=-1.0, scalar2=None,
            op0=ALU.mult))
        ac_ex = p.a(lambda e: e.activation(out=relw[:, 8:12], in_=relw[:, 0:4],
                                           func=AF.Exp, bias=relw[:, 5:6]),
                    waits=[ev_nm])
        p.v(lambda e: e.reduce_sum(out=relw[:, 6:7], in_=relw[:, 8:12], axis=AX.X),
            waits=[ac_ex])
        p.v(lambda e: e.reciprocal(out=relw[:, 7:8], in_=relw[:, 6:7]))
        ev_wt = p.v(lambda e: e.tensor_scalar(
            out=relw[:, 8:12], in0=relw[:, 8:12], scalar1=relw[:, 7:8],
            scalar2=None, op0=ALU.mult))
        ac_g = p.a(lambda e: e.activation(out=small[0:1, 64:80], in_=gate_row[:],
                                          func=AF.Sigmoid), waits=cp_bar)
        ev_og = p.v(lambda e: e.tensor_scalar(
            out=small[0:1, 80:96], in0=small[0:1, 64:80], scalar1=-1.0, scalar2=1.0,
            op0=ALU.mult, op1=ALU.add), waits=[ac_g])
        pe_gc = p.mm(pb[1][0:16, 8:9], small[0:1, 64:80], one_f[:], True, True,
                     waits=[ac_g], inc=s_pe)
        ev_gc = p.v(lambda e: e.tensor_copy(out=relw[:, 12:13], in_=pb[1][0:16, 8:9]),
                    waits=[pe_gc])
        ev_At = p.v(lambda e: e.tensor_scalar(
            out=relw[:, 8:12], in0=relw[:, 8:12], scalar1=relw[:, 12:13],
            scalar2=None, op0=ALU.mult), waits=[ev_wt, ev_gc])
        for h in range(H):
            pe_A = p.mm(pb[2][:, 16 + h * 4:16 + (h + 1) * 4],
                        E16_sb[0:16, h * 128:(h + 1) * 128], relw[:, 8:12],
                        h == 0, False, waits=([ev_At, ld_e16] if h == 0 else None))
        pe_A = p.mm(pb[2][:, 0:16], ones_f[:], small[0:1, 80:96], False, True,
                    waits=[ev_og], inc=s_pe)
        ev_A = p.v(lambda e: e.tensor_copy(out=BG[:], in_=pb[2][:, 0:16]),
                   waits=[pe_A])
        for h in range(H):
            for r4 in range(4):
                ev_A = p.v(lambda e, h=h, r4=r4: e.tensor_copy(
                    out=A_rep[:, h * 16 + r4 * 4:h * 16 + (r4 + 1) * 4],
                    in_=pb[2][:, 16 + h * 4:16 + (h + 1) * 4]))
        bga = carve(AR, 114560, (128,), F32)
        p.v(lambda e: e.memset(bga, 1.0))
        for h in range(H):
            p.v(lambda e, h=h: e.tensor_copy(
                out=bga[:, h * 8:h * 8 + 1], in_=BG[:, h:h + 1]))
            bga_done = p.v(lambda e, h=h: e.tensor_copy(
                out=bga[:, h * 8 + 4:h * 8 + 8], in_=A_rep[:, h * 16:h * 16 + 4]))
        cp_done = [ev_A, bga_done, (s_vf, p.count(s_vf)), (s_ac, p.count(s_ac))]
        if "sbase" in dbg:
            p.dma("sync", dbg["sbase"].ap(), kvz_base, s_st,
                  waits=[(s_vf, p.count(s_vf))])
        if "wts" in dbg:
            p.dma("sync", dbg["wts"].ap(), relw[:, 8:12], s_st,
                  waits=[(s_vf, p.count(s_vf))])
        if "arep" in dbg:
            p.dma("sync", dbg["arep"].ap(), A_rep[:], s_st,
                  waits=[(s_vf, p.count(s_vf))])
        if "bg" in dbg:
            p.dma("sync", dbg["bg"].ap(), BG[:], s_st,
                  waits=[(s_vf, p.count(s_vf))])

        qkv_sb = [carve(AR, 24832, (8, 2052), BF16),
                  carve(AR, 57664, (8, 2052), BF16)]
        mem_rhs = carve(AR, 90496, (8192,), BF16)
        acc = carve(AR, 106880, (512,), F32)
        am23 = carve(AR, 108928, (1024,), BF16)
        attn_bf = carve(AR, 110976, (512,), BF16)
        attn_st = carve(AR, 112000, (512,), BF16)
        p2s = carve(AR, 113024, (64,), F32)
        sT_bf_t = carve(AR, 113280, (128,), BF16)
        attn_bf2 = carve(AR, 113536, (512,), BF16)

        ld_mem = p.dma("gpsimd", mem_rhs.rearrange("p (m h e) -> p m h e", m=M, h=H),
                       d_mem.ap().rearrange("m h d e -> d m h e"), s_ldg,
                       waits=cp_bar)

        Spsum = pb[0]
        numb = pb[1]
        denb = pb[2]
        sTb = pb[3]
        sTb_bf = pb[3].ap().bitcast(BF16)
        amb = [pb[4], pb[5], pb[6], pbf.ap().bitcast(F32)]
        attn_2 = [attn_bf, attn_bf2]
        done_qkv = [None, None]
        st_attn = None
        ev_mask = None
        t2e_prev = None
        ac_cast = None
        attn_done = [None, None]
        chmax_m = [None] * 4
        scale_m = [None] * 4
        mult_m = [None] * 4
        madd_m = [[None] * 4 for _ in range(4)]
        local_m = [None] * 4
        pend_out = [None]
        pend_ev = [None]

        def emit_evac(eh, eg4, eci, ebuf, pkv):
            chmax_m[eci] = p.v(lambda e, eci=eci: e.tensor_scalar(
                out=p2s[:, eci * 8:(eci + 1) * 8],
                in0=denb[:, eci * 8:(eci + 1) * 8], scalar1=EPS_A,
                scalar2=None, op0=ALU.max), waits=[pkv])
            rec = p.v(lambda e, eci=eci: e.reciprocal(
                out=p2s[:, 32 + eci * 8:40 + eci * 8],
                in_=p2s[:, eci * 8:(eci + 1) * 8]),
                waits=([mult_m[eci]] if mult_m[eci] else []))
            wmul = [rec]
            if eh == 0 and eg4 == 0:
                wmul += cp_done
            mult_m[eci] = p.g(lambda e, eci=eci, eh=eh: e.tensor_tensor(
                out=p2s[:, 32 + eci * 8:40 + eci * 8],
                in0=p2s[:, 32 + eci * 8:40 + eci * 8],
                in1=bga[:, eh * 8:(eh + 1) * 8], op=ALU.mult), waits=wmul)
            madd_m[eci][0] = p.v(lambda e, eci=eci: e.tensor_scalar(
                out=acc[:, eci * 128:(eci + 1) * 128],
                in0=amb[0][:, eci * 128:(eci + 1) * 128],
                scalar1=p2s[:, 36 + eci * 8:37 + eci * 8], scalar2=None,
                op0=ALU.mult), waits=[mult_m[eci]])
            for m in range(1, M):
                madd_m[eci][m] = p.v(lambda e, eci=eci, m=m:
                    e.scalar_tensor_tensor(
                        out=acc[:, eci * 128:(eci + 1) * 128],
                        in0=amb[m][:, eci * 128:(eci + 1) * 128],
                        scalar=p2s[:, 36 + eci * 8 + m:37 + eci * 8 + m],
                        in1=acc[:, eci * 128:(eci + 1) * 128],
                        op0=ALU.mult, op1=ALU.add))
            wloc = [attn_done[ebuf]] if attn_done[ebuf] else []
            local_m[eci] = p.v(lambda e, eci=eci, ebuf=ebuf:
                e.scalar_tensor_tensor(
                    out=attn_2[ebuf][:, eci * 128:(eci + 1) * 128],
                    in0=numb[:, eci * 128:(eci + 1) * 128],
                    scalar=p2s[:, 32 + eci * 8:33 + eci * 8],
                    in1=acc[:, eci * 128:(eci + 1) * 128],
                    op0=ALU.mult, op1=ALU.add), waits=wloc or None)
            scale_m[eci] = local_m[eci]
            if eci == 3:
                pend_out[0] = (eh, eg4, ebuf, list(local_m))

        def emit_out():
            nonlocal st_attn, t2e_prev
            hh, g4o, buf, miles = pend_out[0]
            pend_out[0] = None
            pe_t = None
            for ci in range(4):
                w = [miles[ci]]
                if ci == 0 and t2e_prev:
                    w.append(t2e_prev)
                pe_t = p.tp(sTb_bf[:, 256 + ci * 128:256 + (ci + 1) * 128],
                            attn_2[buf][:, ci * 128:(ci + 1) * 128], identb[:],
                            waits=w, inc=(s_pe if ci == 3 else None))
            t2e_prev = p.a(lambda e: e.copy(out=attn_st[:, 0:512],
                                            in_=sTb_bf[:, 256:768]),
                           waits=[pe_t] + ([st_attn] if st_attn else []))
            st_attn = p.dma("sync", d_attn.ap()[hh][:, g4o * 512:(g4o + 1) * 512],
                            attn_st[:, 0:512], s_st, waits=[t2e_prev])
            attn_done[buf] = pe_t

        for h in range(H):
            hp = h % 2
            ldq = p.dma("sync", qkv_sb[hp],
                        d_qkv.ap()[h].rearrange("g p c -> p g c"), s_ld,
                        waits=([done_qkv[hp]] if done_qkv[hp] else
                               [(s_st, p.count(s_st))]))
            pe_init = p.mm(Spsum[:, 0:129], identf[:],
                           kvz_base[:, h * 129:(h + 1) * 129], True, False,
                           waits=([ev0m, (s_vf, p.count(s_vf))] if h == 0 else
                                  [ac_cast]), inc=s_pe)
            pe_kv = pe_init
            ac_cast = p.a(lambda e: e.copy(out=S_bf[0][:], in_=Spsum[:, 0:129]),
                          waits=[pe_init])
            for g4 in range(8):
                buf = (h * 8 + g4) % 2
                for ci in range(4):
                    n = g4 * 4 + ci
                    qp_v = qkv_sb[hp][:, g4, ci * 128:(ci + 1) * 128]
                    kpf_v = qkv_sb[hp][:, g4, 512 + ci * 128:512 + (ci + 1) * 128]
                    kpt_v = qkv_sb[hp][:, g4, 1024 + ci * 128:1024 + (ci + 1) * 128]
                    va_v = qkv_sb[hp][:, g4, 1536 + ci * 129:1536 + (ci + 1) * 129]
                    v_v = qkv_sb[hp][:, g4, 1536 + ci * 129:1536 + ci * 129 + 128]
                    wsT = ([ev_mask] if ev_mask else [])
                    if g4 == 0 and ci == 0:
                        wsT.append(ldq)
                    pe_sT = p.mm(sTb[:, 0:128], kpf_v, qp_v, True, True,
                                 waits=wsT or None, inc=s_pe)
                    mfn = lambda e: e.tensor_tensor(
                        out=sT_bf_t, in0=sTb[:, 0:128], in1=trimask[:],
                        op=ALU.mult)
                    ev_mask = p.v(mfn, waits=[pe_sT])
                    if pend_ev[0] is not None:
                        emit_evac(*pend_ev[0])
                        pend_ev[0] = None
                    if ci == 2 and pend_out[0] is not None:
                        emit_out()
                    for m in range(M):
                        wam = []
                        if ci == 0 and h == 0 and g4 == 0 and m == 0:
                            wam.append(ld_mem)
                        if madd_m[ci][m]:
                            wam.append(madd_m[ci][m])
                        p.mm(amb[m][:, ci * 128:(ci + 1) * 128], qp_v,
                             mem_rhs[:, ((m * H) + h) * 128:((m * H) + h + 1) * 128],
                             ci == 0, False, waits=wam or None)
                    wint = [ac_cast] + ([scale_m[ci]] if scale_m[ci] else [])
                    p.mm(numb[:, ci * 128:(ci + 1) * 128], qp_v,
                         S_bf[n % 2][:, 0:128], ci == 0, False, waits=wint)
                    p.mm(denb[:, ci * 8:ci * 8 + 1], qp_v, S_bf[n % 2][:, 128:129],
                         ci == 0, False,
                         waits=([chmax_m[ci]] if chmax_m[ci] else None))
                    p.mm(denb[:, ci * 8 + 4:ci * 8 + 8], qp_v,
                         norms_bf[:, h * 4:(h + 1) * 4], False, False,
                         waits=([ldg_norm] if h == 0 and g4 == 0 and ci == 0
                                else None))
                    p.mm(numb[:, ci * 128:(ci + 1) * 128], sT_bf_t, v_v,
                         False, False, waits=[ev_mask])
                    p.mm(denb[:, ci * 8:ci * 8 + 1], sT_bf_t, ones_bf[:],
                         False, False)
                    pe_kv = p.mm(Spsum[:, 0:129], kpt_v, va_v, False,
                                 h == H - 1 and n == NCH - 1, inc=s_pe)
                    if n < NCH - 1:
                        ac_cast = p.a(lambda e, nb=(n + 1) % 2: e.copy(
                            out=S_bf[nb][:], in_=Spsum[:, 0:129]), waits=[pe_kv])
                    pend_ev[0] = (h, g4, ci, buf, pe_kv)
            done_qkv[hp] = pe_kv
        emit_evac(*pend_ev[0])
        pend_ev[0] = None
        emit_out()

        if "attn" in dbg:
            p.dma("gpsimd", dbg["attn"].ap(), d_attn.ap(), s_st,
                  waits=[(s_st, p.count(s_st))])

        p4_bar = [(s_st, p.count(s_st)), (s_pe, p.count(s_pe)),
                  (s_vf, p.count(s_vf)), (s_ac, p.count(s_ac)),
                  (s_pf, p.count(s_pf))]
        wo_bf = carve(AR, 0, (16, 2048), BF16)
        att4 = [carve(AR, 65536, (16, 128), BF16), carve(AR, 69632, (16, 128), BF16)]
        xc4 = [carve(AR, 73728, (2048,), F32), carve(AR, 81920, (2048,), F32)]
        yc4 = [carve(AR, 90112, (2048,), BF16), carve(AR, 94208, (2048,), BF16)]
        oc4 = [carve(AR, 98304, (2048,), F32), carve(AR, 106496, (2048,), F32)]

        ld_wo = p.dma("gpsimd", wo_bf,
                      d_wo.ap().rearrange("(k p) c -> p k c", p=128), s_ldg,
                      waits=p4_bar)
        done_a4 = [None, None]
        done_o4 = [None, None]
        last_out = None
        pey4 = [None] * 8
        done_p4pb = [None] * 8
        for c in range(NCH):
            bp = c % 2
            lda = p.dma("sync", att4[bp],
                        d_attn.ap()[:, :, c * 128:(c + 1) * 128]
                        .rearrange("h p c -> p h c"), s_ld,
                        waits=([done_a4[bp]] if done_a4[bp] else p4_bar))
            ldx = p.dma("sync", xc4[bp], d_x.ap()[c * 128:(c + 1) * 128, :], s_ld)
            ldy = p.dma("sync", yc4[bp], d_y.ap()[c * 128:(c + 1) * 128, :], s_ld,
                        waits=([y_done[c]]))
            bs = 3 * (c % 2)
            for o in range(4):
                for k in range(ND):
                    w0_ = None
                    if k == 0:
                        w0_ = [lda, ld_wo] +                               ([done_p4pb[bs + o]] if done_p4pb[bs + o] else [])
                    pey4[o] = p.mm(pb[bs + o][:, 0:512], att4[bp][:, k, :],
                                   wo_bf[:, k, o * 512:(o + 1) * 512],
                                   k == 0, k == ND - 1, waits=w0_,
                                   inc=(s_pe if k == ND - 1 else None))
            wo_done = pey4[3]
            for o in range(4):
                ev_s1 = p.v(lambda e, bs=bs, o=o, bp=bp: e.tensor_tensor(
                    out=oc4[bp][:, o * 512:(o + 1) * 512], in0=pb[bs + o][:, 0:512],
                    in1=xc4[bp][:, o * 512:(o + 1) * 512], op=ALU.add),
                    waits=[pey4[o], ldx] + ([done_o4[bp]] if (o == 0 and done_o4[bp])
                                            else []))
                done_p4pb[bs + o] = ev_s1
                po_s2 = p.g(lambda e, o=o, bp=bp: e.tensor_tensor(
                    out=oc4[bp][:, o * 512:(o + 1) * 512],
                    in0=oc4[bp][:, o * 512:(o + 1) * 512],
                    in1=yc4[bp][:, o * 512:(o + 1) * 512], op=ALU.add),
                    waits=[ev_s1, ldy])
            last_out = p.dma("sync", d_out.ap()[c * 128:(c + 1) * 128, :],
                             oc4[bp], s_out, waits=[po_s2])
            done_o4[bp] = last_out
            done_a4[bp] = wo_done
        p.op("gpsimd", lambda e: e.nop(), waits=[last_out], nochain=True)

        block = ctx.enter_context(nc.Block())
        p.emit(block)
    return nc


def _host_prep(inputs):
    x = np.ascontiguousarray(inputs["L0_attn_hidden_states"].reshape(B * S, D))
    shared = {
        "wq_t": np.ascontiguousarray(inputs["L0_attn_w_q"].T),
        "wk_t": np.ascontiguousarray(inputs["L0_attn_w_k"].T),
        "wv_t": np.ascontiguousarray(inputs["L0_attn_w_v"].T),
        "wo_t": np.ascontiguousarray(inputs["L0_attn_w_o"].T),
        "memories": np.ascontiguousarray(inputs["L0_attn_memories"]),
        "memnorms_t": np.ascontiguousarray(
            inputs["L0_attn_memory_norms"].transpose(2, 1, 0).reshape(HD, H * M)),
        "gate_row": np.ascontiguousarray(inputs["L0_attn_gate"].reshape(1, H)),
        "gate_wT": np.ascontiguousarray(inputs["L0_ffn_gate_w"].T),
        "ebias_row": np.ascontiguousarray(inputs["L0_ffn_expert_bias"].reshape(1, E)),
        "w1t": np.ascontiguousarray(inputs["L0_ffn_w1"].transpose(0, 2, 1)),
        "w3t": np.ascontiguousarray(inputs["L0_ffn_w3"].transpose(0, 2, 1)),
        "w2t": np.ascontiguousarray(inputs["L0_ffn_w2"].transpose(0, 2, 1)),
        "sw1t": np.ascontiguousarray(inputs["L0_ffn_sw1"].T),
        "sw3t": np.ascontiguousarray(inputs["L0_ffn_sw3"].T),
        "sw2t": np.ascontiguousarray(inputs["L0_ffn_sw2"].T),
        "identb": np.eye(128, dtype=BF),
        "identf": np.eye(128, dtype=np.float32),
        "trimask": np.triu(np.ones((128, 128), np.float32)).astype(BF),
        "E16": np.repeat(np.eye(16, dtype=np.float32)[:, :, None], 128, axis=2)
              .reshape(16, 16 * 128),
        "E8": np.repeat(np.eye(8, dtype=np.float32)[:, :, None], 128, axis=2)
              .reshape(8, 8 * 128).astype(BF),
    }
    in_maps = []
    for c in range(NCORES):
        b, q = divmod(c, 4)
        sl = x[c * T:(c + 1) * T]
        pref = np.zeros(8, np.float32)
        rl = np.zeros(8, np.float32)
        for qq in range(4):
            rl[b * 4 + qq] = 1.0
            if qq < q:
                pref[b * 4 + qq] = 1.0
        m = dict(shared)
        m["x_tm"] = sl
        m["x_fm"] = np.ascontiguousarray(sl.T)
        m["prefmask"] = pref.reshape(1, 8)
        m["relmask"] = rl.reshape(1, 8)
        in_maps.append(m)
    return in_maps


def kernel(**inputs):
    in_maps = _host_prep(inputs)
    nc = build()
    res = run_bass_kernel_spmd(nc, in_maps, core_ids=list(range(NCORES)))
    out = np.concatenate([r["out"] for r in res.results], axis=0)
    return out.reshape(B, S, D)


def run_debug(inputs, cores=(0,)):
    in_maps = _host_prep(inputs)
    nc = build()
    res = run_bass_kernel_spmd(nc, [in_maps[c] for c in cores], core_ids=list(cores))
    return res.results
